# version 20
# baseline (speedup 1.0000x reference)
import contextlib
import math
import numpy as np
import concourse.bass as bass
import concourse.mybir as mybir
from concourse.bass_utils import run_bass_kernel_spmd

F32 = mybir.dt.float32
BF16 = mybir.dt.bfloat16
AF = mybir.ActivationFunctionType
ALU = mybir.AluOpType
AX = mybir.AxisListType.X

MAGIC = 1.5 * 2.0 ** 23
INV2PI = float(1.0 / (2.0 * math.pi))
TWOPI = float(2.0 * math.pi)

D = 1024
TOT = 2080
K = 32
H = 32
MM_ = 4
FF = 4096
NC17 = 17
CHUNK = [128] * 16 + [32]
EPS = 1e-5

ENGINES = ("sync", "vector", "scalar", "tensor", "gpsimd")


class _LastInst:

    def __init__(self, eng):
        self._e = eng
        self.last = None

    def __getattr__(self, n):
        a = getattr(self._e, n)
        if callable(a):
            def call(*args, _a=a, **kw):
                r = _a(*args, **kw)
                if hasattr(r, "then_inc"):
                    self.last = r
                return r
            return call
        return a


class _EstInst:
    def then_inc(self, *a, **k):
        return self


def _ap_free(ap):
    try:
        sh = ap.shape
    except Exception:
        return 256
    n = 1
    for d in sh[1:]:
        n *= d
    return max(n, 1)


def _ap_16bit(ap):
    try:
        return "16" in str(ap.dtype)
    except Exception:
        return False


class _Est:

    def __init__(self, eng):
        self.eng = eng
        self.t = 0.0

    def __getattr__(self, name):
        def call(*args, **kw):
            t = 0.0
            out = kw.get("out")
            if out is None and args:
                out = args[0]
            if name == "matmul":
                rhs = kw.get("rhs", args[2] if len(args) > 2 else None)
                n = _ap_free(rhs) if rhs is not None else 256
                t = 45 + n * 0.45
            elif name == "transpose":
                t = 140 + 128 * 0.45
            elif name in ("drain", "nop", "engine_nop"):
                t = 40
            elif name.startswith("dma"):
                t = 120
            elif name in ("wait_ge",):
                t = 20
            else:
                f = _ap_free(out) if out is not None else 128
                if self.eng == "scalar":
                    t = 190 + f * 0.85
                elif self.eng == "gpsimd":
                    t = 300 + f * 4.0
                else:
                    rate = 1.08
                    if name in ("tensor_scalar", "tensor_copy", "memset",
                                "tensor_scalar_mul", "tensor_scalar_add",
                                "tensor_relu", "copy"):
                        rate = 0.55
                    if out is not None and _ap_16bit(out):
                        rate *= 0.7
                    t = 80 + f * rate
            self.t += t
            return _EstInst()
        return call


class _Node:
    __slots__ = ("i", "eng", "body", "preds", "kind", "ndma", "est", "seg")

    def __init__(self, i, eng, body, kind, ndma, est, seg):
        self.i = i
        self.eng = eng
        self.body = body
        self.preds = set()
        self.kind = kind
        self.ndma = ndma
        self.est = est
        self.seg = seg


class B:

    def __init__(self, nc, stack):
        self.nc = nc
        self.sem = {}
        for e in ENGINES:
            self.sem[e] = stack.enter_context(nc.semaphore(f"sp_{e}"))
        self.NHD = 8
        self.NGD = 4
        self.dpool = [stack.enter_context(nc.semaphore(f"dma_h{i}"))
                      for i in range(self.NHD)]
        self.gpool = [stack.enter_context(nc.semaphore(f"dma_g{i}"))
                      for i in range(self.NGD)]
        self.nodes = []
        self.last_w = {}
        self.readers = {}
        self.seg = 0
        self.ev = {e: [] for e in ENGINES}

    def _add(self, eng, body, reads, writes, kind, ndma, est):
        n = _Node(len(self.nodes), eng, body, kind, ndma, est, self.seg)
        for bnm in list(reads) + list(writes):
            p = self.last_w.get(bnm)
            if p is not None:
                n.preds.add(p)
        for bnm in writes:
            for p in self.readers.get(bnm, ()):
                n.preds.add(p)
        n.preds.discard(n.i)
        self.nodes.append(n)
        for bnm in reads:
            self.readers.setdefault(bnm, []).append(n.i)
        for bnm in writes:
            self.last_w[bnm] = n.i
            self.readers[bnm] = []
        return n

    def stage(self, eng, body, reads=(), writes=()):
        est = _Est(eng)
        try:
            body(est)
        except Exception:
            est.t = 1000.0
        self._add(eng, body, reads, writes, "stage", 0, est.t)

    def dma(self, body, n, eng="sync", reads=(), writes=()):
        self._add(eng, body, reads, writes,
                  "dma" if eng == "sync" else "gdma", n, 2000.0 + n * 1500.0)

    def barrier(self):
        self.seg += 1

    def finish(self):
        pass

    def _schedule(self):
        nodes = self.nodes
        nseg = self.seg + 1
        order = []
        eng_t = {e: 0.0 for e in ENGINES}
        finish = [0.0] * len(nodes)
        for sg in range(nseg):
            segnodes = [n for n in nodes if n.seg == sg]
            if not segnodes:
                continue
            base = max(eng_t.values())
            for e in eng_t:
                eng_t[e] = base
            indeg = {}
            succs = {}
            segset = {n.i for n in segnodes}
            for n in segnodes:
                cnt = 0
                for p in n.preds:
                    if p in segset:
                        cnt += 1
                        succs.setdefault(p, []).append(n.i)
                indeg[n.i] = cnt
            ready = [n.i for n in segnodes if indeg[n.i] == 0]
            nmap = {n.i: n for n in segnodes}
            import heapq
            done = 0
            while ready:
                best = None
                best_key = None
                for i in ready:
                    n = nmap[i]
                    st = eng_t[n.eng]
                    for p in n.preds:
                        if p in segset and finish[p] > st:
                            st = finish[p]
                    key = (st, i)
                    if best_key is None or key < best_key:
                        best_key = key
                        best = i
                ready.remove(best)
                n = nmap[best]
                st = best_key[0]
                if n.kind == "stage":
                    finish[best] = st + n.est
                    eng_t[n.eng] = st + n.est
                else:
                    eng_t[n.eng] = st + 300.0
                    finish[best] = st + n.est
                order.append(best)
                for s2 in succs.get(best, ()):
                    indeg[s2] -= 1
                    if indeg[s2] == 0:
                        ready.append(s2)
            assert len(order) == sum(1 for n in nodes if n.seg <= sg), "cycle"
        return order

    def _emit_all(self):
        order = self._schedule()
        nodes = self.nodes
        seqno = [0] * len(nodes)
        cnt = {e: 0 for e in ENGINES}
        dseq = [None] * len(nodes)
        dpc = [0] * self.NHD
        gpc = [0] * self.NGD
        drr = grr = 0
        for i in order:
            n = nodes[i]
            if n.kind == "stage":
                cnt[n.eng] += 1
                seqno[i] = cnt[n.eng]
            elif n.kind == "dma":
                k = drr % self.NHD
                drr += 1
                dseq[i] = ("hd", k, dpc[k], dpc[k] + 16 * n.ndma)
                dpc[k] += 16 * n.ndma
            else:
                k = grr % self.NGD
                grr += 1
                dseq[i] = ("gd", k, gpc[k], gpc[k] + 16 * n.ndma)
                gpc[k] += 16 * n.ndma

        def tok(i):
            n = nodes[i]
            if n.kind == "stage":
                return (n.eng, seqno[i])
            kd, k, prev, val = dseq[i]
            return (f"{kd}{k}", val)

        def semof(kind):
            if kind.startswith("hd"):
                return self.dpool[int(kind[2:])]
            if kind.startswith("gd"):
                return self.gpool[int(kind[2:])]
            return self.sem[kind]

        known = {e: {} for e in ENGINES}
        running = {}
        cur_seg = 0
        for i in order:
            n = nodes[i]
            if n.seg != cur_seg:
                for e in ENGINES:
                    kn = known[e]
                    for kd, v in running.items():
                        if kd == e or kn.get(kd, 0) >= v:
                            continue
                        self.ev[e].append(
                            lambda en, s=semof(kd), v=v: en.wait_ge(s, v))
                        kn[kd] = v
                cur_seg = n.seg
            eng = n.eng
            kn = known[eng]
            need = {}
            for p in n.preds:
                kd, v = tok(p)
                if kd == eng:
                    continue
                if need.get(kd, 0) < v:
                    need[kd] = v
            pre = []
            for kd in sorted(need):
                v = need[kd]
                if kn.get(kd, 0) >= v:
                    continue
                pre.append((semof(kd), v))
                kn[kd] = v
            if n.kind == "stage":
                def wrapped(e, pre=pre, body=n.body, eng=eng,
                            s=self.sem[eng]):
                    for sm, v in pre:
                        e.wait_ge(sm, v)
                    if eng == "tensor":
                        prox = _LastInst(e)
                        body(prox)
                        assert prox.last is not None
                        prox.last.then_inc(s, 1)
                    else:
                        body(e)
                        e.drain().then_inc(s, 1)
                self.ev[eng].append(wrapped)
                kd, v = tok(i)
                running[kd] = v
            else:
                kd, k, prev, val = dseq[i]
                kdk = f"{kd}{k}"
                sm = semof(kdk)
                if prev and kn.get(kdk, 0) < prev:
                    pre.append((sm, prev))
                    kn[kdk] = prev
                def wrappedd(e, pre=pre, body=n.body, sm=sm):
                    for s2, v in pre:
                        e.wait_ge(s2, v)
                    body(e, sm)
                self.ev[eng].append(wrappedd)
                running[kdk] = val
        for k, c in enumerate(dpc):
            if c:
                self.ev["sync"].append(
                    lambda e, s=self.dpool[k], v=c: e.wait_ge(s, v))
        for k, c in enumerate(gpc):
            if c:
                self.ev["sync"].append(
                    lambda e, s=self.gpool[k], v=c: e.wait_ge(s, v))

    def emit(self, block):
        self._emit_all()
        for name in ENGINES:
            evs = self.ev[name]

            def section(eng, evs=evs):
                for fn in evs:
                    fn(eng)

            getattr(block, name)(section)


def col(t, j):
    return t[:, j:j + 1]


def build_nc(L=4096, T=256):
    NT = L // T
    NSUB = T // 128
    nc = bass.Bass("TRN2", target_bir_lowering=False)

    dram = {}

    def din(name, shape, dtype=F32):
        dram[name] = nc.dram_tensor(name, list(shape), dtype, kind="ExternalInput")
        return dram[name]

    din("x", (L, D))
    din("in_proj_w", (D, TOT))
    din("out_proj_w", (D, D))
    din("w1T", (D, FF))
    din("w2T", (FF, D))
    din("b1", (1, FF))
    din("b2", (1, D))
    din("c_cols", (128, 8))
    din("wmod", (D, 6 * D))
    din("wconv_cols", (128, 4 * NC17))
    din("convb_cols", (128, NC17))
    din("th_all", (128, K))
    din("ss_col", (K, 1))
    din("slopes_col", (K, 1))
    din("dl_row", (1, 3))
    din("nsBD", (128, 8))
    din("BDraw", (128, 8 * 128))
    din("ident", (128, 128), BF16)
    din("ident_f", (128, 128))
    din("ones_row", (1, 128))
    din("ones_col", (128, 1))
    din("Sel", (K, 8 * 128))
    din("GS", (128, 8 * K), BF16)
    din("iotaD", (K, L))

    out_d = nc.dram_tensor("out", [L, D], F32, kind="ExternalOutput")
    xmid = nc.dram_tensor("xmid", [L, D], F32, kind="Internal")
    logtwd = nc.dram_tensor("logtwd", [K, L], F32, kind="Internal")

    ap = {k: v.ap() for k, v in dram.items()}

    with contextlib.ExitStack() as S0:
        b = B(nc, S0)

        def sb(name, shape, dtype=F32, stack=S0):
            return stack.enter_context(nc.sbuf_tensor(name, list(shape), dtype))

        ident = sb("identb", (128, 128), BF16)
        ident_f = sb("identf", (128, 128))
        ones_r = sb("ones_r", (1, 128))
        ones_c = sb("ones_c", (128, 1))
        gs_sb = sb("gs_sb", (128, 8 * K), BF16)
        bd_sb = sb("bd_sb", (128, 8 * 128), BF16)
        th_sb = sb("th_sb", (128, K))
        wcv_sb = sb("wcv_sb", (128, 4 * NC17))
        cbv_sb = sb("cbv_sb", (128, NC17))
        ss_sb = sb("ss_sb", (K, 1))
        slp_sb = sb("slp_sb", (K, 1))
        dl_sb = sb("dl_sb", (1, 3))
        w012 = sb("w012", (128, 3))
        sa_c = sb("sa_c", (128, 8))
        ca1_c = sb("ca1_c", (128, 8))

        cb1B = sb("cb1B", (128, D))
        gaB = sb("gaB", (128, D))
        sbB = sb("sbB", (128, D))
        gbB = sb("gbB", (128, D))

        b2gb_r = sb("b2gb_r", (1, D), BF16)
        b1_r = sb("b1_r", (1, FF), BF16)
        x_sb = [[sb(f"x_sb{p}{s}", (128, D)) for s in range(NSUB)] for p in range(2)]
        xo_sb = [sb(f"xo_sb{s}", (128, D)) for s in range(NSUB)]
        smallc = sb("smallc", (128, 16))
        srow = sb("srow", (1, 16))

        def ld_consts(e, sem):
            for name, dst in [("ident", ident), ("ident_f", ident_f),
                              ("ones_row", ones_r), ("ones_col", ones_c),
                              ("GS", gs_sb), ("th_all", th_sb),
                              ("wconv_cols", wcv_sb), ("convb_cols", cbv_sb),
                              ("ss_col", ss_sb), ("slopes_col", slp_sb),
                              ("dl_row", dl_sb)]:
                e.dma_start(out=dst[:], in_=ap[name]).then_inc(sem, 16)
        b.dma(ld_consts, 11, reads=("SU",), writes=("SU", "ident", "identf", "ones", "gs", "th", "wcv", "cbv", "ss", "slp", "dl"))

        def s_theta(e):
            e.tensor_scalar(out=th_sb[:], in0=th_sb[:], scalar1=INV2PI,
                            scalar2=None, op0=ALU.mult)
        b.stage("vector", s_theta, reads=("SU", "th"), writes=("SU", "th"))

        def s_sp1(e):
            e.activation(out=smallc[0:K, 0:1], in_=slp_sb[:], func=AF.Abs)
        b.stage("scalar", s_sp1, reads=("SU",), writes=("SU",))

        def s_sp2(e):
            e.activation(out=smallc[0:K, 1:2], in_=smallc[0:K, 0:1], func=AF.Exp,
                         scale=-1.0)
        b.stage("scalar", s_sp2, reads=("SU",), writes=("SU",))

        def s_sp3(e):
            e.tensor_scalar(out=smallc[0:K, 2:3], in0=smallc[0:K, 1:2],
                            scalar1=1.0, scalar2=None, op0=ALU.add)
        b.stage("vector", s_sp3, reads=("SU",), writes=("SU",))

        def s_sp4(e):
            e.activation(out=smallc[0:K, 3:4], in_=smallc[0:K, 2:3], func=AF.Ln)
        b.stage("scalar", s_sp4, reads=("SU",), writes=("SU",))

        def s_sp5(e):
            e.tensor_relu(out=smallc[0:K, 4:5], in_=slp_sb[:])
            e.drain()
            e.tensor_tensor(out=smallc[0:K, 5:6], in0=smallc[0:K, 4:5],
                            in1=smallc[0:K, 3:4], op=ALU.add)
            e.drain()
            e.tensor_scalar(out=smallc[0:K, 6:7], in0=smallc[0:K, 5:6],
                            scalar1=-1.0, scalar2=None, op0=ALU.mult)
        b.stage("vector", s_sp5, reads=("SU",), writes=("SU", "smallc"))

        def s_dl1(e):
            e.reduce_max(out=srow[0:1, 0:1], in_=dl_sb[:], axis=AX)
            e.drain()
            e.tensor_scalar(out=srow[0:1, 1:4], in0=dl_sb[:],
                            scalar1=srow[0:1, 0:1], scalar2=None, op0=ALU.subtract)
        b.stage("vector", s_dl1, reads=("SU",), writes=("SU",))

        def s_dl2(e):
            e.activation(out=srow[0:1, 4:7], in_=srow[0:1, 1:4], func=AF.Exp)
        b.stage("scalar", s_dl2, reads=("SU",), writes=("SU",))

        def s_dl3(e):
            e.reduce_sum(out=srow[0:1, 7:8], in_=srow[0:1, 4:7], axis=AX)
            e.drain()
            e.reciprocal(out=srow[0:1, 8:9], in_=srow[0:1, 7:8])
            e.drain()
            e.tensor_scalar(out=srow[0:1, 9:12], in0=srow[0:1, 4:7],
                            scalar1=srow[0:1, 8:9], scalar2=None, op0=ALU.mult)
        b.stage("vector", s_dl3, reads=("SU",), writes=("SU", "srow"))

        StRows = contextlib.ExitStack()
        row_ga = sb("row_ga", (1, D), F32, StRows)
        row_sb = sb("row_sb", (1, D), F32, StRows)
        row_cb1 = sb("row_cb1", (1, D), F32, StRows)
        row_gb = sb("row_gb", (1, D), F32, StRows)
        rscr = sb("rscr", (1, 2 * D), F32, StRows)
        with contextlib.ExitStack() as St0:
            psW = St0.enter_context(nc.psum_tensor("psW", [128, 16], F32))
            psMC = St0.enter_context(nc.psum_tensor("psMC", [128, 48], F32))
            psTr = St0.enter_context(nc.psum_tensor("psTr", [1, 128], F32))
            wmstg = [St0.enter_context(nc.sbuf_tensor(f"wmstg{i}", [128, 6 * D], F32))
                     for i in range(2)]
            siluc = St0.enter_context(nc.sbuf_tensor("siluc", [128, 8], F32))
            cvec = St0.enter_context(nc.sbuf_tensor("cvec", [128, 8], F32))
            mcols = St0.enter_context(nc.sbuf_tensor("mcols", [128, 48], F32))
            b.stage("vector", lambda e: e.memset(mcols[:], 0.0), reads=("SU",), writes=("SU",))

            def s_w012mm(e):
                e.matmul(psW[:, 0:3], ones_r[:], srow[0:1, 9:12],
                         start=True, stop=True)
            b.stage("tensor", s_w012mm, reads=("SU", "srow", "ones"), writes=("SU",))

            def s_w012ev(e):
                e.tensor_copy(out=w012[:], in_=psW[:, 0:3])
                e.drain()
                e.tensor_scalar(out=w012[:, 2:3], in0=w012[:, 2:3],
                                scalar1=-1.0, scalar2=None, op0=ALU.mult)
            b.stage("vector", s_w012ev, reads=("SU",), writes=("SU", "w012"))

            b.dma(lambda e, sem: e.dma_start(out=cvec[:], in_=ap["c_cols"])
                  .then_inc(sem, 16), 1, reads=("SU",), writes=("SU",))
            b.stage("scalar", lambda e: e.activation(out=siluc[:], in_=cvec[:],
                                                     func=AF.Silu),
                    reads=("SU",), writes=("SU",))

            for fc in range(8):
                stg = wmstg[fc % 2]
                b.dma(lambda e, sem, fc=fc, stg=stg:
                      e.dma_start(out=stg[:], in_=ap["wmod"][fc * 128:(fc + 1) * 128, :])
                      .then_inc(sem, 16), 1, reads=("SU",), writes=("SU",))

                def s_modmm(e, fc=fc, stg=stg):
                    for oc in range(48):
                        e.matmul(psMC[:, oc:oc + 1], stg[:, oc * 128:(oc + 1) * 128],
                                 siluc[:, fc:fc + 1], start=True, stop=True)
                b.stage("tensor", s_modmm, reads=("SU",), writes=("SU",))

                def s_macc(e):
                    e.tensor_tensor(out=mcols[:], in0=mcols[:], in1=psMC[:],
                                    op=ALU.add)
                b.stage("vector", s_macc, reads=("SU",), writes=("SU",))

            def s_modev(e):
                e.tensor_copy(out=sa_c[:], in_=mcols[:, 0:8])
                e.tensor_scalar(out=ca1_c[:], in0=mcols[:, 8:16], scalar1=1.0,
                                scalar2=None, op0=ALU.add)
            b.stage("vector", s_modev, reads=("SU",), writes=("SU", "sa", "ca1"))

            for gi, dst in ((0, row_ga), (1, row_sb), (2, row_cb1), (3, row_gb)):
                for j in range(8):
                    def s_rt(e, gi=gi, j=j):
                        e.transpose(psTr[:, :], mcols[:, 16 + gi * 8 + j:17 + gi * 8 + j],
                                    ident_f[:])
                    b.stage("tensor", s_rt, reads=("SU", "identf"), writes=("SU",))

                    def s_rte(e, dst=dst, j=j):
                        e.tensor_copy(out=dst[0:1, j * 128:(j + 1) * 128],
                                      in_=psTr[0:1, :])
                    b.stage("vector", s_rte, reads=("SU",), writes=("SU",))

            for rowt in (row_ga, row_gb):
                def s_gabs(e, rowt=rowt):
                    e.activation(out=rscr[0:1, 0:D], in_=rowt[0:1, :], func=AF.Abs)
                b.stage("scalar", s_gabs, reads=("SU",), writes=("SU",))

                def s_gfix(e, rowt=rowt):
                    e.tensor_scalar(out=rscr[0:1, D:2 * D], in0=rowt[0:1, :],
                                    scalar1=0.0, scalar2=None, op0=ALU.is_ge)
                    e.tensor_scalar(out=rscr[0:1, 0:D], in0=rscr[0:1, 0:D],
                                    scalar1=0.25, scalar2=None, op0=ALU.max)
                    e.drain()
                    e.tensor_scalar(out=rscr[0:1, D:2 * D], in0=rscr[0:1, D:2 * D],
                                    scalar1=2.0, scalar2=-1.0,
                                    op0=ALU.mult, op1=ALU.add)
                    e.drain()
                    e.tensor_tensor(out=rowt[0:1, :], in0=rscr[0:1, D:2 * D],
                                    in1=rscr[0:1, 0:D], op=ALU.mult)
                b.stage("vector", s_gfix, reads=("SU",), writes=("SU",))

            def s_cb1(e):
                e.tensor_scalar(out=row_cb1[0:1, :], in0=row_cb1[0:1, :],
                                scalar1=1.0, scalar2=None, op0=ALU.add)
            b.stage("vector", s_cb1, reads=("SU",), writes=("SU",))

        with contextlib.ExitStack() as St1:
            psB = St1.enter_context(nc.psum_tensor("psB", [128, D], F32))
            bdstg = St1.enter_context(nc.sbuf_tensor("bdstg", [128, D], F32))
            nsbd = St1.enter_context(nc.sbuf_tensor("nsbd", [128, 8], F32))
            b2row = St1.enter_context(nc.sbuf_tensor("b2row", [1, D], F32))
            b2gbf = St1.enter_context(nc.sbuf_tensor("b2gbf", [1, D], F32))

            for rowt, dst in ((row_ga, gaB), (row_sb, sbB),
                              (row_cb1, cb1B), (row_gb, gbB)):
                def s_bmm(e, rowt=rowt):
                    e.matmul(psB[:, 0:512], ones_r[:], rowt[0:1, 0:512],
                             start=True, stop=True)
                    e.matmul(psB[:, 512:1024], ones_r[:], rowt[0:1, 512:1024],
                             start=True, stop=True)
                b.stage("tensor", s_bmm, reads=("SU", "ones"), writes=("SU",))

                def s_bev(e, dst=dst):
                    e.tensor_copy(out=dst[:], in_=psB[:])
                b.stage("vector", s_bev, reads=("SU",), writes=("SU", "gaB", "sbB", "cb1B", "gbB"))

            b.dma(lambda e, sem: e.dma_start(out=b2row[:], in_=ap["b2"])
                  .then_inc(sem, 16), 1, reads=("SU",), writes=("SU",))

            def s_b2gb(e):
                e.tensor_tensor(out=b2gbf[:], in0=b2row[:], in1=row_gb[0:1, :],
                                op=ALU.mult)
                e.drain()
                e.tensor_copy(out=b2gb_r[:], in_=b2gbf[:])
            b.stage("vector", s_b2gb, reads=("SU",), writes=("SU", "b2gbr"))

            b.dma(lambda e, sem: e.dma_start(out=b1_r[:], in_=ap["b1"])
                  .then_inc(sem, 16), 1, eng="gpsimd", reads=("SU",), writes=("SU", "b1r"))

            b.dma(lambda e, sem: e.dma_start(out=nsbd[:], in_=ap["nsBD"])
                  .then_inc(sem, 16), 1, reads=("SU",), writes=("SU",))
            for blk in range(8):
                b.dma(lambda e, sem, blk=blk:
                      e.dma_start(out=bdstg[:, 0:128],
                                  in_=ap["BDraw"][:, blk * 128:(blk + 1) * 128])
                      .then_inc(sem, 16), 1, reads=("SU",), writes=("SU",))

                def s_bd(e, blk=blk):
                    e.tensor_scalar(out=bd_sb[:, blk * 128:(blk + 1) * 128],
                                    in0=bdstg[:, 0:128], scalar1=col(nsbd, blk),
                                    scalar2=None, op0=ALU.mult)
                b.stage("vector", s_bd, reads=("SU",), writes=("SU", "bd"))

        StRows.close()
        b.barrier()

        with contextlib.ExitStack() as SA:
            logtw_t = [sb(f"logtw{i}", (K, T), F32, SA) for i in range(2)]
            sel_sb = sb("sel_sb", (K, 8 * 128), F32, SA)
            win_bf = [sb(f"win{i}", (128, TOT), BF16, SA) for i in range(8)]
            wout_bf = [sb(f"wout{i}", (128, D), BF16, SA) for i in range(8)]
            psT = S0.enter_context(nc.psum_tensor("psT", [128, 1024], BF16))
            psY = S0.enter_context(nc.psum_tensor("psY", [128, 2048], F32))
            psQ = S0.enter_context(nc.psum_tensor("psQT", [128, 256], F32))
            psZ = SA.enter_context(nc.psum_tensor("psZ", [128, 1024], F32))
            psRp = [psZ[:, 0:256], psZ[:, 512:768]]

            b.dma(lambda e, sem: e.dma_start(out=sel_sb[:], in_=ap["Sel"])
                  .then_inc(sem, 16), 1, reads=("SU",), writes=("SU", "sel"))
            if True:
                SWA = contextlib.ExitStack()
                wstA = sb("wstA", (128, D), F32, SWA)
                lt_stg = sb("lt_stg", (K, 1024), F32, SWA)
                for lc in range(L // 1024 if L >= 1024 else 1):
                    lw = min(1024, L)
                    b.dma(lambda e, sem, lc=lc, lw=lw:
                          e.dma_start(out=lt_stg[:, 0:lw],
                                      in_=ap["iotaD"][:, lc * 1024:lc * 1024 + lw])
                          .then_inc(sem, 16), 1, reads=("SU",), writes=("SU",))

                    def s_lt(e, lc=lc, lw=lw):
                        e.tensor_scalar(out=lt_stg[:, 0:lw], in0=lt_stg[:, 0:lw],
                                        scalar1=smallc[0:K, 6:7], scalar2=None,
                                        op0=ALU.mult)
                    b.stage("vector", s_lt, reads=("SU", "smallc"), writes=("SU",))
                    b.dma(lambda e, sem, lc=lc, lw=lw:
                          e.dma_start(out=logtwd.ap()[:, lc * 1024:lc * 1024 + lw],
                                      in_=lt_stg[:, 0:lw]).then_inc(sem, 16),
                          1, reads=("SU",), writes=("SU",))

            for fc in range(8):
                b.dma(lambda e, sem, fc=fc:
                      e.dma_start(out=win_bf[fc][:],
                                  in_=ap["in_proj_w"][fc * 128:(fc + 1) * 128, :])
                      .then_inc(sem, 16), 1, eng="gpsimd", writes=("win",))
            for fc in range(8):
                b.dma(lambda e, sem, fc=fc:
                      e.dma_start(out=wstA[:],
                                  in_=ap["out_proj_w"][fc * 128:(fc + 1) * 128, :])
                      .then_inc(sem, 16), 1, reads=("wstA",), writes=("wstA",))

                def s_wo(e, fc=fc):
                    e.tensor_tensor(out=wout_bf[fc][:], in0=wstA[:], in1=gaB[:],
                                    op=ALU.mult)
                b.stage("vector", s_wo, reads=("wstA", "gaB"), writes=("wstA", "wout"))

            SWA.close()
            b.barrier()

            h_bf = [sb(f"hbf{i}", (128, T), BF16, SA) for i in range(8)]
            z_sb = [sb(f"z{i}", (128, 3 + T), F32, SA) for i in range(NC17)]
            z2_sb = [sb(f"zz{i}", (128, T), F32, SA) for i in range(NC17)]
            pw_sb = sb("pw_sb", (K, T), F32, SA)
            cpw_sb = sb("cpw_sb", (K, T), F32, SA)
            dcarry = sb("dcarry", (K, 1), F32, SA)
            dD_sb = sb("dD_sb", (K, T), F32, SA)
            d2s_sb = dD_sb
            sarg_sb = dD_sb
            sS_sb = dD_sb
            gate_bf = [sb(f"gate{i}", (128, T), BF16, SA) for i in range(8)]
            _pb = sb("polyb", (128, T), F32, SA)
            polyb = [_pb, _pb]
            g_bf = [sb(f"g_bf{i}", (128, T), BF16, SA) for i in range(2)]
            _g2 = sb("gn2", (128, MM_ * T), F32, SA)
            gn2p = [_g2, _g2]
            fbuf = [sb(f"fbuf{i}", (128, MM_ * T), F32, SA) for i in range(2)]
            nbuf = [sb(f"nbuf{i}", (128, MM_ * T), F32, SA) for i in range(2)]
            f25 = [sb(f"f25_{i}", (128, MM_ * T), F32, SA) for i in range(2)]
            sin_bf = [sb(f"sin_bf{i}", (128, MM_ * T), BF16, SA) for i in range(2)]
            cos_bf = [sb(f"cos_bf{i}", (128, MM_ * T), BF16, SA) for i in range(2)]
            are_bf = [sb(f"are_bf{i}", (128, MM_ * T), BF16, SA) for i in range(2)]
            aim_bf = [sb(f"aim_bf{i}", (128, MM_ * T), BF16, SA) for i in range(2)]
            ncar_re = sb("ncar_re", (128, K), F32, SA)
            ncar_im = sb("ncar_im", (128, K), F32, SA)
            _yc = sb("ycp_bf", (128, T), BF16, SA)
            _sr = sb("srep_bf", (128, T), BF16, SA)
            ycp_bf = [_yc, _yc]
            srep_bf = [_sr, _sr]
            lnc = sb("lnc", (128, 12), F32, SA)
            xn_bf = [sb(f"xnbf{s}", (128, D), BF16, SA) for s in range(NSUB)]


            def s_zero(e):
                for ocx in range(NC17):
                    e.memset(z_sb[ocx][:, 0:3], 0.0)
                e.memset(ncar_re[:], 0.0)
                e.memset(ncar_im[:], 0.0)
                e.memset(dcarry[:], 0.0)
            b.stage("vector", s_zero, writes=tuple(f"z{i}" for i in range(NC17)) + ("ncar", "dcarry"))

            b.barrier()

            def emit_headA(ti):
                r0 = ti * T
                tpar = ti % 2

                def s_ldx(e, sem, r0=r0, tpar=tpar):
                    for s in range(NSUB):
                        e.dma_start(out=x_sb[tpar][s][:],
                                    in_=ap["x"][r0 + s * 128:r0 + (s + 1) * 128, :]) \
                            .then_inc(sem, 16)
                b.dma(s_ldx, NSUB, writes=(f"x{tpar}0", f"x{tpar}1"))
                b.dma(lambda e, sem, r0=r0, tpar=tpar:
                      e.dma_start(out=logtw_t[tpar][:],
                                  in_=logtwd.ap()[:, r0:r0 + T]).then_inc(sem, 16),
                      1, writes=(f"logtw{tpar}",))

                def s_ln1(e, tpar=tpar):
                    for s in range(NSUB):
                        e.reduce_sum(out=col(lnc, s * 6 + 0), in_=x_sb[tpar][s][:], axis=AX)
                        e.scalar_tensor_tensor(out=(fbuf[0] if s == 0 else nbuf[0])[:, 0:D],
                                               in0=x_sb[tpar][s][:],
                                               scalar=1.0, in1=x_sb[tpar][s][:],
                                               op0=ALU.mult, op1=ALU.mult,
                                               accum_out=col(lnc, s * 6 + 1))
                    e.drain()
                    for s in range(NSUB):
                        e.tensor_scalar(out=col(lnc, s * 6 + 2),
                                        in0=col(lnc, s * 6 + 0),
                                        scalar1=1.0 / D, scalar2=None, op0=ALU.mult)
                    e.drain()
                    for s in range(NSUB):
                        e.tensor_tensor(out=col(lnc, s * 6 + 3),
                                        in0=col(lnc, s * 6 + 2),
                                        in1=col(lnc, s * 6 + 2), op=ALU.mult)
                    e.drain()
                    for s in range(NSUB):
                        e.scalar_tensor_tensor(out=col(lnc, s * 6 + 4),
                                               in0=col(lnc, s * 6 + 1),
                                               scalar=1.0 / D,
                                               in1=col(lnc, s * 6 + 3),
                                               op0=ALU.mult, op1=ALU.subtract)
                    e.drain()
                    for s in range(NSUB):
                        e.tensor_scalar(out=col(lnc, s * 6 + 4),
                                        in0=col(lnc, s * 6 + 4),
                                        scalar1=EPS, scalar2=None, op0=ALU.add)
                b.stage("vector", s_ln1, reads=(f"x{tpar}0", f"x{tpar}1"), writes=("lnc", "f0", "n0"))

                def s_rstd1(e):
                    for s in range(NSUB):
                        e.activation(out=col(lnc, s * 6 + 5),
                                     in_=col(lnc, s * 6 + 4), func=AF.Ln)
                    e.drain()
                    for s in range(NSUB):
                        e.activation(out=col(lnc, s * 6 + 5),
                                     in_=col(lnc, s * 6 + 5), func=AF.Exp,
                                     scale=-0.5)
                b.stage("scalar", s_rstd1, reads=("lnc",), writes=("lnc",))

                def s_xn(e, tpar=tpar):
                    for s in range(NSUB):
                        e.tensor_scalar(out=xn_bf[s][:], in0=x_sb[tpar][s][:],
                                        scalar1=col(lnc, s * 6 + 2),
                                        scalar2=col(lnc, s * 6 + 5),
                                        op0=ALU.subtract, op1=ALU.mult)
                b.stage("vector", s_xn, reads=(f"x{tpar}0", f"x{tpar}1", "lnc"), writes=("xn",))


            def emit_headB(ti):
                r0 = ti * T
                tpar = ti % 2
                for hh in range(2):
                    def s_tp1(e, hh=hh):
                        for j in range(4):
                            ocx = hh * 4 + j
                            for s in range(NSUB):
                                e.transpose(psT[:, (j * NSUB + s) * 128:
                                                (j * NSUB + s + 1) * 128],
                                            xn_bf[s][:, ocx * 128:(ocx + 1) * 128],
                                            ident[:])
                    b.stage("tensor", s_tp1, reads=("xn", "ident"), writes=("psT",))

                    def s_hfm(e, hh=hh):
                        for j in range(4):
                            ocx = hh * 4 + j
                            e.tensor_scalar(out=h_bf[ocx][:],
                                            in0=psT[:, j * NSUB * 128:
                                                    (j + 1) * NSUB * 128],
                                            scalar1=col(ca1_c, ocx),
                                            scalar2=col(sa_c, ocx),
                                            op0=ALU.mult, op1=ALU.add)
                    b.stage("vector", s_hfm, reads=("psT", "sa", "ca1"),
                            writes=("hbf",))

                for pz in range(9):
                    chunks = [c for c in (2 * pz, 2 * pz + 1) if c < NC17]

                    def s_zmm(e, chunks=chunks):
                        for ocz in chunks:
                            w = CHUNK[ocz]
                            pzl = psZ[0:w, (ocz % 2) * 512:(ocz % 2) * 512 + T]
                            for fc in range(8):
                                e.matmul(pzl,
                                         win_bf[fc][:, ocz * 128:ocz * 128 + w],
                                         h_bf[fc][:], start=(fc == 0),
                                         stop=(fc == 7))
                    b.stage("tensor", s_zmm, reads=("hbf", "win"),
                            writes=("psZa", "psZb"))

                    def s_zev(e, chunks=chunks):
                        for ocz in chunks:
                            w = CHUNK[ocz]
                            e.copy(out=z_sb[ocz][0:w, 3:3 + T],
                                   in_=psZ[0:w, (ocz % 2) * 512:
                                           (ocz % 2) * 512 + T])
                    b.stage("scalar", s_zev, reads=("psZa", "psZb"),
                            writes=tuple(f"z{c}" for c in chunks))

                def s_cv0(e):
                    for ocz in range(NC17):
                        w = CHUNK[ocz]
                        e.tensor_scalar(out=z2_sb[ocz][0:w, :],
                                        in0=z_sb[ocz][0:w, 0:T],
                                        scalar1=wcv_sb[0:w, ocz * 4:ocz * 4 + 1],
                                        scalar2=cbv_sb[0:w, ocz:ocz + 1],
                                        op0=ALU.mult, op1=ALU.add)
                b.stage("vector", s_cv0, reads=tuple(f"z{i}" for i in range(NC17)) + ("wcv", "cbv"), writes=("zz",))
                for dd in range(1, 4):
                    def s_cvd(e, dd=dd):
                        for ocz in range(NC17):
                            w = CHUNK[ocz]
                            e.scalar_tensor_tensor(
                                out=z2_sb[ocz][0:w, :],
                                in0=z_sb[ocz][0:w, dd:dd + T],
                                scalar=wcv_sb[0:w, ocz * 4 + dd:ocz * 4 + dd + 1],
                                in1=z2_sb[ocz][0:w, :],
                                op0=ALU.mult, op1=ALU.add)
                    b.stage("vector", s_cvd, reads=tuple(f"z{i}" for i in range(NC17)) + ("wcv", "zz"), writes=("zz",))

                def s_hist(e):
                    for ocz in range(NC17):
                        w = CHUNK[ocz]
                        e.tensor_copy(out=z_sb[ocz][0:w, 0:3],
                                      in_=z_sb[ocz][0:w, T:T + 3])
                b.stage("vector", s_hist, reads=tuple(f"z{i}" for i in range(NC17)), writes=tuple(f"z{i}" for i in range(NC17)))

                def s_sc1(e, tpar=tpar):
                    sr = z2_sb[16][0:K, :]
                    e.tensor_scalar(out=pw_sb[:], in0=sr, scalar1=ss_sb[:, 0:1],
                                    scalar2=None, op0=ALU.mult)
                    e.drain()
                    e.tensor_scalar(out=pw_sb[:], in0=pw_sb[:], scalar1=20.0,
                                    scalar2=-20.0, op0=ALU.min, op1=ALU.max)
                    e.drain()
                    e.tensor_tensor(out=pw_sb[:], in0=pw_sb[:],
                                    in1=logtw_t[tpar][:], op=ALU.add)
                b.stage("vector", s_sc1, reads=("zz", f"logtw{tpar}", "ss"), writes=("pw",))

                b.stage("scalar", lambda e: e.activation(out=pw_sb[:], in_=pw_sb[:],
                                                         func=AF.Exp),
                        reads=("pw",), writes=("pw",))

                def s_den(e):
                    e.tensor_tensor_scan(out=cpw_sb[:], data0=pw_sb[:],
                                         data1=pw_sb[:], initial=dcarry[:, 0:1],
                                         op0=ALU.add, op1=ALU.bypass)
                    e.drain()
                    e.tensor_copy(out=dcarry[:, 0:1], in_=cpw_sb[:, T - 1:T])
                    e.tensor_scalar(out=dD_sb[:], in0=cpw_sb[:], scalar1=1e-4,
                                    scalar2=None, op0=ALU.max)
                b.stage("vector", s_den, reads=("pw", "dcarry"), writes=("cpw", "dcarry", "dDx"))

                def s_d2s(e):
                    e.activation(out=d2s_sb[:], in_=dD_sb[:], func=AF.Square,
                                 scale=float(math.sqrt(1e-5)))
                    for ocx in range(8):
                        e.activation(out=gate_bf[ocx][:], in_=z2_sb[8 + ocx][:],
                                     func=AF.Silu)
                b.stage("scalar", s_d2s, reads=("dDx", "zz"), writes=("dDx", "gate"))


            def emit_tail(ti):
                r0 = ti * T
                tpar = ti % 2
                def g_pre(oc):
                    P_ = oc % 2
                    psZn = "psZa" if P_ == 0 else "psZb"
                    def s_pwrep(e, oc=oc, P_=P_):
                        e.matmul(psRp[P_][:, 0:T],
                                 sel_sb[:, oc * 128:(oc + 1) * 128],
                                 pw_sb[:], start=True, stop=True)
                    b.stage("tensor", s_pwrep, reads=("sel", "pw"), writes=(psZn,))

                    def s_poly(e, oc=oc, P_=P_):
                        e.tensor_scalar(out=polyb[P_][:], in0=z2_sb[oc][:],
                                        scalar1=col(w012, 2), scalar2=col(w012, 1),
                                        op0=ALU.mult, op1=ALU.add)
                        e.drain()
                        e.tensor_tensor(out=polyb[P_][:], in0=z2_sb[oc][:],
                                        in1=polyb[P_][:], op=ALU.mult)
                        e.drain()
                        e.scalar_tensor_tensor(out=g_bf[P_][:], in0=polyb[P_][:],
                                               scalar=col(w012, 0),
                                               in1=psRp[P_][:, 0:T],
                                               op0=ALU.add, op1=ALU.mult)
                        for m in range(MM_):
                            e.tensor_scalar(out=fbuf[P_][:, m * T:(m + 1) * T],
                                            in0=z2_sb[oc][:],
                                            scalar1=th_sb[:, oc * 4 + m:oc * 4 + m + 1],
                                            scalar2=None, op0=ALU.mult)
                    b.stage("vector", s_poly,
                            reads=("zz", psZn, "w012", "th"),
                            writes=("poly0", f"g{P_}", f"f{P_}"))

                    def s_magic(e, P_=P_):
                        e.activation(out=nbuf[P_][:], in_=fbuf[P_][:], func=AF.Copy,
                                     bias=MAGIC)
                        e.drain()
                        e.activation(out=nbuf[P_][:], in_=nbuf[P_][:], func=AF.Copy,
                                     bias=-MAGIC)
                        e.activation(out=f25[P_][:], in_=fbuf[P_][:], func=AF.Copy,
                                     bias=0.25)
                        e.drain()
                        e.activation(out=gn2p[P_][:], in_=f25[P_][:], func=AF.Copy,
                                     bias=MAGIC)
                        e.drain()
                        e.activation(out=gn2p[P_][:], in_=gn2p[P_][:], func=AF.Copy,
                                     bias=-MAGIC)
                    b.stage("scalar", s_magic, reads=(f"f{P_}",),
                            writes=(f"n{P_}", f"f25_{P_}", "gn2x"))


                def g_mid(oc):
                    P_ = oc % 2
                    def s_rr(e, P_=P_):
                        e.tensor_tensor(out=nbuf[P_][:], in0=fbuf[P_][:], in1=nbuf[P_][:],
                                        op=ALU.subtract)
                        e.tensor_tensor(out=f25[P_][:], in0=f25[P_][:], in1=gn2p[P_][:],
                                        op=ALU.subtract)
                    b.stage("vector", s_rr,
                            reads=(f"f{P_}", f"n{P_}", f"f25_{P_}", "gn2x"),
                            writes=(f"n{P_}", f"f25_{P_}"))

                    def s_sincos(e, oc=oc, P_=P_):
                        e.activation(out=sin_bf[P_][:], in_=nbuf[P_][:], func=AF.Sin,
                                     scale=TWOPI)
                        e.activation(out=cos_bf[P_][:], in_=f25[P_][:], func=AF.Sin,
                                     scale=TWOPI)
                    b.stage("scalar", s_sincos, reads=(f"n{P_}", f"f25_{P_}"),
                            writes=(f"sin{P_}", f"cos{P_}"))


                def g_post(oc):
                    P_ = oc % 2
                    def s_aplanes(e, oc=oc, P_=P_):
                        for m in range(MM_):
                            e.tensor_tensor(out=are_bf[P_][:, m * T:(m + 1) * T],
                                            in0=cos_bf[P_][:, m * T:(m + 1) * T],
                                            in1=g_bf[P_][:], op=ALU.mult)
                            e.tensor_tensor(out=aim_bf[P_][:, m * T:(m + 1) * T],
                                            in0=sin_bf[P_][:, m * T:(m + 1) * T],
                                            in1=g_bf[P_][:], op=ALU.mult)
                    b.stage("vector", s_aplanes,
                            reads=(f"sin{P_}", f"cos{P_}", f"g{P_}"),
                            writes=(f"are{P_}", f"aim{P_}"))

                    def s_scan(e, oc=oc, P_=P_):
                        for m in range(MM_):
                            e.tensor_tensor_scan(
                                out=are_bf[P_][:, m * T:(m + 1) * T],
                                data0=are_bf[P_][:, m * T:(m + 1) * T],
                                data1=are_bf[P_][:, m * T:(m + 1) * T],
                                initial=col(ncar_re, oc * 4 + m),
                                op0=ALU.add, op1=ALU.bypass)
                            e.tensor_tensor_scan(
                                out=aim_bf[P_][:, m * T:(m + 1) * T],
                                data0=aim_bf[P_][:, m * T:(m + 1) * T],
                                data1=aim_bf[P_][:, m * T:(m + 1) * T],
                                initial=col(ncar_im, oc * 4 + m),
                                op0=ALU.add, op1=ALU.bypass)
                        e.drain()
                        e.tensor_copy(out=ncar_re[:, oc * 4:oc * 4 + 4],
                                      in_=are_bf[P_][:, T - 1::T])
                        e.tensor_copy(out=ncar_im[:, oc * 4:oc * 4 + 4],
                                      in_=aim_bf[P_][:, T - 1::T])
                    b.stage("vector", s_scan,
                            reads=(f"are{P_}", f"aim{P_}", "ncar"),
                            writes=(f"are{P_}", f"aim{P_}", "ncar", f"car{P_}"))

                    def s_sq(e, oc=oc, P_=P_):
                        e.activation(out=sin_bf[P_][:], in_=are_bf[P_][:], func=AF.Square)
                        e.activation(out=cos_bf[P_][:], in_=aim_bf[P_][:], func=AF.Square)
                    b.stage("scalar", s_sq, reads=(f"are{P_}", f"aim{P_}"),
                            writes=(f"sin{P_}", f"cos{P_}"))

                    def s_wpp(e, oc=oc, P_=P_):
                        yp = psY[:, oc * 256:oc * 256 + T]
                        for m in range(MM_):
                            e.matmul(yp, bd_sb[:, m * 128:(m + 1) * 128],
                                     are_bf[P_][:, m * T:(m + 1) * T],
                                     start=(m == 0), stop=False)
                        for m in range(MM_):
                            e.matmul(yp, bd_sb[:, (4 + m) * 128:(5 + m) * 128],
                                     aim_bf[P_][:, m * T:(m + 1) * T],
                                     start=False, stop=(m == 3))
                        qp = psQ[0:K, 0:T]
                        first = oc == 0
                        for m in range(MM_):
                            e.matmul(qp, gs_sb[:, oc * K:(oc + 1) * K],
                                     sin_bf[P_][:, m * T:(m + 1) * T],
                                     start=(first and m == 0), stop=False)
                        for m in range(MM_):
                            e.matmul(qp, gs_sb[:, oc * K:(oc + 1) * K],
                                     cos_bf[P_][:, m * T:(m + 1) * T],
                                     start=False, stop=(oc == 7 and m == 3))
                    b.stage("tensor", s_wpp,
                            reads=("bd", "gs", f"are{P_}", f"aim{P_}",
                                   f"sin{P_}", f"cos{P_}"),
                            writes=(f"psY{oc}", "psQ"))


                g_pre(0)
                g_mid(0)
                for oc in range(8):
                    if oc + 1 < 8:
                        g_pre(oc + 1)
                    g_post(oc)
                    if oc + 1 < 8:
                        g_mid(oc + 1)

                def s_sarg(e):
                    e.scalar_tensor_tensor(out=sarg_sb[:], in0=psQ[0:K, 0:T],
                                           scalar=1.0 / 256.0, in1=d2s_sb[:],
                                           op0=ALU.mult, op1=ALU.add)
                b.stage("vector", s_sarg, reads=("psQ", "dDx"), writes=("dDx",))

                def s_sln(e):
                    e.activation(out=sS_sb[:], in_=sarg_sb[:], func=AF.Ln)
                    e.drain()
                    e.activation(out=sS_sb[:], in_=sS_sb[:], func=AF.Exp, scale=-0.5)
                b.stage("scalar", s_sln, reads=("dDx",), writes=("dDx",))

                for oc in range(8):
                    P_ = oc % 2
                    psZn = "psZa" if P_ == 0 else "psZb"

                    def s_srep(e, oc=oc, P_=P_):
                        e.matmul(psRp[P_][:, 0:T],
                                 sel_sb[:, oc * 128:(oc + 1) * 128],
                                 sS_sb[:], start=True, stop=True)
                    b.stage("tensor", s_srep, reads=("sel", "dDx"), writes=(psZn,))

                    def s_uevc(e, oc=oc, P_=P_):
                        e.copy(out=ycp_bf[P_][:], in_=psY[:, oc * 256:oc * 256 + T])
                        e.drain()
                        e.copy(out=srep_bf[P_][:], in_=psRp[P_][:, 0:T])
                    b.stage("scalar", s_uevc,
                            reads=(f"psY{oc}", psZn),
                            writes=("ycp0", "srep0"))

                    def s_uev(e, oc=oc, P_=P_):
                        e.tensor_tensor(out=ycp_bf[P_][:], in0=ycp_bf[P_][:],
                                        in1=srep_bf[P_][:], op=ALU.mult)
                        e.drain()
                        e.tensor_tensor(out=gate_bf[oc][:], in0=ycp_bf[P_][:],
                                        in1=gate_bf[oc][:], op=ALU.mult)
                    b.stage("vector", s_uev,
                            reads=("ycp0", "srep0", "gate"),
                            writes=("ycp0", "gate", f"u{oc}"))

                for s in range(NSUB):
                    for nh in range(2):
                        def s_op(e, s=s, nh=nh):
                            pa = psY[:, (s * 2 + nh) * 512:(s * 2 + nh) * 512 + 512]
                            for fc in range(8):
                                e.matmul(pa, gate_bf[fc][:, s * 128:(s + 1) * 128],
                                         wout_bf[fc][:, nh * 512:(nh + 1) * 512],
                                         start=(fc == 0), stop=(fc == 7))
                        b.stage("tensor", s_op, reads=tuple(f"u{i}" for i in range(8)) + ("wout",), writes=(f"psY{(s * 2 + nh) * 2}", f"psY{(s * 2 + nh) * 2 + 1}"))

                def s_res1_0(e, tpar=tpar):
                    e.tensor_tensor(
                        out=xo_sb[0][:, 0 * 512:(0 + 1) * 512],
                        in0=x_sb[tpar][0][:, 0 * 512:(0 + 1) * 512],
                        in1=psY[:, 0 * 512:0 * 512 + 512],
                        op=ALU.add)
                b.stage("vector", s_res1_0,
                        reads=(f"x{tpar}0", "psY0", "psY1"),
                        writes=("xo0",))

                def s_res1_1(e, tpar=tpar):
                    e.tensor_tensor(
                        out=xo_sb[0][:, 1 * 512:(1 + 1) * 512],
                        in0=x_sb[tpar][0][:, 1 * 512:(1 + 1) * 512],
                        in1=psY[:, 1 * 512:1 * 512 + 512],
                        op=ALU.add)
                b.stage("vector", s_res1_1,
                        reads=(f"x{tpar}0", "psY2", "psY3"),
                        writes=("xo0",))

                def s_res1_2(e, tpar=tpar):
                    e.tensor_tensor(
                        out=xo_sb[1][:, 0 * 512:(0 + 1) * 512],
                        in0=x_sb[tpar][1][:, 0 * 512:(0 + 1) * 512],
                        in1=psY[:, 2 * 512:2 * 512 + 512],
                        op=ALU.add)
                b.stage("vector", s_res1_2,
                        reads=(f"x{tpar}1", "psY4", "psY5"),
                        writes=("xo1",))

                def s_res1_3(e, tpar=tpar):
                    e.tensor_tensor(
                        out=xo_sb[1][:, 1 * 512:(1 + 1) * 512],
                        in0=x_sb[tpar][1][:, 1 * 512:(1 + 1) * 512],
                        in1=psY[:, 3 * 512:3 * 512 + 512],
                        op=ALU.add)
                b.stage("vector", s_res1_3,
                        reads=(f"x{tpar}1", "psY6", "psY7"),
                        writes=("xo1",))

                def s_stx(e, sem, r0=r0):
                    for s in range(NSUB):
                        e.dma_start(out=xmid.ap()[r0 + s * 128:r0 + (s + 1) * 128, :],
                                    in_=xo_sb[s][:]).then_inc(sem, 16)
                b.dma(s_stx, NSUB, reads=("xo0", "xo1"), writes=("xmid",))


            emit_headA(0)
            emit_headB(0)
            for ti in range(NT):
                if ti + 1 < NT:
                    emit_headA(ti + 1)
                emit_tail(ti)
                if ti + 1 < NT:
                    emit_headB(ti + 1)

        with contextlib.ExitStack() as SB:
            w1q = [sb(f"w1q{i}", (128, FF), BF16, SB) for i in range(8)]
            w2q = [sb(f"w2q{i}", (128, D), BF16, SB) for i in range(32)]
            wstage = sb("wstage", (128, 512), F32, SB)
            qc = sb("qc", (128, 10), F32, SB)
            h_act = sb("h_act", (128, FF), BF16, SB)
            h2_sb = sb("h2_sb", (128, D), F32, SB)
            q1_bf = sb("q1_bf", (128, D), BF16, SB)
            q1f = [sb(f"q1f{i}", (128, 128), BF16, SB) for i in range(8)]
            q2f = [sb(f"q2f{i}", (128, 128), BF16, SB) for i in range(16)]
            c2p = [sb(f"c2_{i}", (128, 20), F32, SB) for i in range(4)]
            iacol1 = [sb(f"iacol1_{i}", (128, 1), BF16, SB) for i in range(4)]
            iacol2 = [sb(f"iacol2_{i}", (128, 1), BF16, SB) for i in range(4)]
            iarp = [sb(f"iar_{i}", (1, 128), BF16, SB) for i in range(4)]
            bnst = sb("bnst", (128, 48), F32, SB)
            bnst2 = sb("bnst2", (128, 48), F32, SB)
            amx = sb("amx", (128, 8), F32, SB)
            al1 = [sb(f"al1_{i}", (128, 1), F32, SB) for i in range(8)]
            al2 = [sb(f"al2_{i}", (128, 1), F32, SB) for i in range(8)]
            fap = [sb(f"fa_{i}", (128, 4), F32, SB) for i in range(4)]
            fbp = [sb(f"fb_{i}", (128, 8), F32, SB) for i in range(4)]
            dap = [sb(f"da_{i}", (128, 4), F32, SB) for i in range(4)]
            dbp = [sb(f"db_{i}", (128, 8), F32, SB) for i in range(4)]
            psT2a = SB.enter_context(nc.psum_tensor("psT2a", [128, 1024], BF16))
            psT2b = SB.enter_context(nc.psum_tensor("psT2b", [128, 1024], BF16))
            psT2 = [psT2a, psT2b]

            def quant_weights(tensors, dram_name, nch, width, scale_slot, do_gb):
                pfx = dram_name
                for fc in range(nch):
                    b.dma(lambda e, sem, fc=fc:
                          e.dma_start(out=tensors[fc][:],
                                      in_=ap[dram_name][fc * 128:(fc + 1) * 128, :])
                          .then_inc(sem, 16), 1, eng="gpsimd",
                          writes=(f"{pfx}_{fc}",))

                ngrp = (nch + 7) // 8
                for gqi in range(ngrp):
                    def s_m1(e, gqi=gqi):
                        for j in range(8):
                            fc = gqi * 8 + j
                            if fc < nch:
                                e.tensor_reduce(out=col(qc, j),
                                                in_=tensors[fc][:], axis=AX,
                                                op=ALU.add,
                                                apply_absolute_value=True)
                    b.stage("vector", s_m1,
                            reads=tuple(f"{pfx}_{gqi * 8 + j}" for j in range(8)
                                        if gqi * 8 + j < nch),
                            writes=("qc",))

                    def s_m2(e, gqi=gqi):
                        e.matmul(psQ[0:1, gqi * 8:gqi * 8 + 8], ones_c[:],
                                 qc[:, 0:8], start=True, stop=True)
                    b.stage("tensor", s_m2, reads=("qc", "ones"), writes=("psQ",))

                def s_m3(e):
                    e.reduce_sum(out=srow[0:1, 0:1], in_=psQ[0:1, 0:nch], axis=AX)
                    e.drain()
                    e.tensor_scalar(out=srow[0:1, 1:2], in0=srow[0:1, 0:1],
                                    scalar1=1.0 / (nch * 128 * width),
                                    scalar2=1e-5, op0=ALU.mult, op1=ALU.max)
                    e.drain()
                    e.reciprocal(out=srow[0:1, 2:3], in_=srow[0:1, 1:2])
                b.stage("vector", s_m3, reads=("psQ",), writes=("srow",))

                def s_bc(e):
                    e.matmul(psQ[:, 8:10], ones_r[:], srow[0:1, 1:3],
                             start=True, stop=True)
                b.stage("tensor", s_bc, reads=("srow", "ones"), writes=("psQ",))

                def s_bc2(e):
                    e.tensor_copy(out=smallc[:, scale_slot:scale_slot + 2],
                                  in_=psQ[:, 8:10])
                b.stage("vector", s_bc2, reads=("psQ",), writes=("smallc",))

                nseg = (width + 511) // 512
                segw = width // nseg
                for fc in range(nch):
                    for sg in range(nseg):
                        def s_q(e, fc=fc, sg=sg):
                            w0 = sg * segw
                            ws = wstage[:, 0:segw]
                            e.tensor_copy(out=ws, in_=tensors[fc][:, w0:w0 + segw])
                            e.drain()
                            e.tensor_scalar(out=ws, in0=ws,
                                            scalar1=col(smallc, scale_slot + 1),
                                            scalar2=MAGIC, op0=ALU.mult, op1=ALU.add)
                            e.drain()
                            e.tensor_scalar(out=ws, in0=ws, scalar1=MAGIC,
                                            scalar2=None, op0=ALU.subtract)
                            e.drain()
                            e.tensor_scalar(out=ws, in0=ws, scalar1=1.0,
                                            scalar2=-1.0, op0=ALU.min, op1=ALU.max)
                            e.drain()
                            if do_gb:
                                e.tensor_tensor(out=ws, in0=ws,
                                                in1=gbB[:, w0:w0 + segw] if segw < D else gbB[:],
                                                op=ALU.mult)
                                e.drain()
                            e.tensor_copy(out=tensors[fc][:, w0:w0 + segw], in_=ws)
                        b.stage("vector", s_q,
                                reads=(f"{pfx}_{fc}", "smallc", "gbB"),
                                writes=(f"{pfx}_{fc}", "wstage"))

            b.barrier()
            quant_weights(w1q, "w1T", 8, FF, 0, False)
            quant_weights(w2q, "w2T", 32, D, 2, True)

            I32 = mybir.dt.int32

            def qrsqrt(e, dst, src, tmp, hx):
                d_i = dst.bitcast(I32)
                e.tensor_scalar(out=d_i, in0=src.bitcast(I32),
                                scalar1=1, scalar2=None,
                                op0=ALU.logical_shift_right)
                e.tensor_scalar(out=hx, in0=src,
                                scalar1=-0.5, scalar2=None, op0=ALU.mult)
                e.drain()
                e.tensor_scalar(out=d_i, in0=d_i, scalar1=-1,
                                scalar2=0x5F3759DF, op0=ALU.mult, op1=ALU.add)
                e.drain()
                for _ in range(2):
                    e.tensor_tensor(out=tmp, in0=dst, in1=dst, op=ALU.mult)
                    e.drain()
                    e.tensor_scalar(out=tmp, in0=tmp,
                                    scalar1=hx, scalar2=1.5,
                                    op0=ALU.mult, op1=ALU.add)
                    e.drain()
                    e.tensor_tensor(out=dst, in0=dst, in1=tmp, op=ALU.mult)
                    e.drain()

            def p2_front(u):
                ti, s = divmod(u, NSUB)
                r0 = ti * T
                tpar = ti % 2
                up = u % 4
                u8 = u % 8
                c2 = c2p[up]
                cn = f"c2_{up}"
                fa, fan = fap[up], f"fa{up}"
                fb, fbn = fbp[up], f"fb{up}"
                ia_row = iarp[up]
                ian = f"ia{up}"

                def s_ldm(e, sem, r0=r0, s=s, tpar=tpar):
                    e.dma_start(out=x_sb[tpar][s][:],
                                in_=xmid.ap()[r0 + s * 128:r0 + (s + 1) * 128, :]) \
                        .then_inc(sem, 16)
                b.dma(s_ldm, 1, reads=("xmid",), writes=(f"x{tpar}{s}",))

                def s_ln2(e, s=s, tpar=tpar, c2=c2):
                    e.bn_stats(out=bnst[:, 0:6], in_=x_sb[tpar][s][:, 0:512])
                    e.bn_stats(out=bnst[:, 6:12], in_=x_sb[tpar][s][:, 512:1024])
                    e.drain()
                    e.bn_aggr(out=c2[:, 2:4], in_=bnst[:, 0:12])
                    e.drain()
                    e.tensor_scalar(out=col(c2, 4), in0=col(c2, 3),
                                    scalar1=EPS, scalar2=None, op0=ALU.add)
                    e.drain()
                    qrsqrt(e, col(c2, 5), col(c2, 4), col(c2, 0), col(c2, 1))
                b.stage("vector", s_ln2, reads=(f"x{tpar}{s}",),
                        writes=(cn, "bnst"))

                def s_h2(e, s=s, tpar=tpar, c2=c2):
                    e.tensor_scalar(out=h2_sb[:], in0=x_sb[tpar][s][:],
                                    scalar1=col(c2, 2), scalar2=col(c2, 5),
                                    op0=ALU.subtract, op1=ALU.mult)
                    e.drain()
                    e.tensor_tensor(out=h2_sb[:], in0=h2_sb[:], in1=cb1B[:],
                                    op=ALU.mult)
                    e.drain()
                    e.tensor_tensor(out=h2_sb[:], in0=h2_sb[:], in1=sbB[:],
                                    op=ALU.add)
                    e.drain()
                    e.bn_stats(out=bnst[:, 0:6], in_=h2_sb[:, 0:512])
                    e.bn_stats(out=bnst[:, 6:12], in_=h2_sb[:, 512:1024])
                    e.reduce_max(out=col(fa, 0), in_=h2_sb[:], axis=AX,
                                 apply_absolute_value=True)
                    e.drain()
                    e.bn_aggr(out=c2[:, 17:19], in_=bnst[:, 0:12])
                    e.drain()
                    e.tensor_tensor(out=col(c2, 6), in0=col(c2, 17),
                                    in1=col(c2, 17), op=ALU.mult)
                    e.drain()
                    e.tensor_tensor(out=col(c2, 6), in0=col(c2, 6),
                                    in1=col(c2, 18), op=ALU.add)
                    e.drain()
                    e.tensor_scalar(out=col(c2, 6), in0=col(c2, 6),
                                    scalar1=EPS, scalar2=None, op0=ALU.add)
                    e.drain()
                    e.tensor_scalar(out=col(fa, 1), in0=col(fa, 0),
                                    scalar1=1e-6, scalar2=None, op0=ALU.max)
                    e.drain()
                    e.reciprocal(out=col(fa, 1), in_=col(fa, 1))
                    e.drain()
                    e.tensor_scalar(out=col(fa, 2), in0=col(fa, 1),
                                    scalar1=127.0, scalar2=None, op0=ALU.mult)
                b.stage("vector", s_h2,
                        reads=(f"x{tpar}{s}", cn, "cb1B", "sbB"),
                        writes=("h2", "bnst", cn, fan))

                def s_q1(e, fa=fa):
                    e.activation(out=h2_sb[:], in_=h2_sb[:], func=AF.Copy,
                                 scale=col(fa, 2), bias=MAGIC)
                    e.drain()
                    e.activation(out=q1_bf[:], in_=h2_sb[:], func=AF.Copy,
                                 bias=-MAGIC)
                b.stage("scalar", s_q1, reads=("h2", fan),
                        writes=("q1", "h2"))

                def s_cs1b(e, c2=c2, fa=fa, fb=fb):
                    qrsqrt(e, col(fb, 2), col(c2, 6), col(fb, 0), col(fb, 1))
                    e.tensor_tensor(out=col(fb, 3), in0=col(fa, 0),
                                    in1=col(fb, 2), op=ALU.mult)
                    e.drain()
                    e.tensor_scalar(out=al1[u8][:, 0:1], in0=col(fb, 3),
                                    scalar1=col(smallc, 0),
                                    scalar2=1.0 / 127.0,
                                    op0=ALU.mult, op1=ALU.mult)
                    e.drain()
                    e.reciprocal(out=col(fb, 4), in_=al1[u8][:, 0:1])
                    e.drain()
                    e.tensor_copy(out=iacol1[up][:], in_=col(fb, 4))
                b.stage("vector", s_cs1b, reads=(cn, fan, "smallc"),
                        writes=(fbn, f"al1_{u8}", f"iac1{up}"))

                def s_iat(e, up=up):
                    e.transpose(psT[0:1, 0:128], iacol1[up][:], ident[:])
                b.stage("tensor", s_iat, reads=(f"iac1{up}", "ident"), writes=("psT",))

                def s_iaev(e, ia_row=ia_row):
                    e.activation(out=ia_row[0:1, :], in_=psT[0:1, 0:128],
                                 func=AF.Copy)
                b.stage("scalar", s_iaev, reads=("psT",), writes=(ian,))

                def s_tq1(e):
                    for fc in range(8):
                        e.transpose(psT[:, fc * 128:(fc + 1) * 128],
                                    q1_bf[:, fc * 128:(fc + 1) * 128], ident[:])
                b.stage("tensor", s_tq1, reads=("q1", "ident"), writes=("psT",))

                def s_tq1e(e):
                    for fc in range(8):
                        e.activation(out=q1f[fc][:],
                                     in_=psT[:, fc * 128:(fc + 1) * 128],
                                     func=AF.Copy)
                b.stage("scalar", s_tq1e, reads=("psT",), writes=("q1f",))

            def p2_mid(u):
                up = u % 4
                u8 = u % 8
                c2 = c2p[up]
                cn = f"c2_{up}"
                ia_row = iarp[up]
                ian = f"ia{up}"
                for nn in range(8):
                    def s_w1mm(e, nn=nn, ia_row=ia_row):
                        ph = psY[:, (nn % 2) * 512:(nn % 2) * 512 + 512]
                        for fc in range(8):
                            e.matmul(ph, q1f[fc][:],
                                     w1q[fc][:, nn * 512:(nn + 1) * 512],
                                     start=(fc == 0), stop=False)
                        e.matmul(ph, ia_row[0:1, :],
                                 b1_r[0:1, nn * 512:(nn + 1) * 512],
                                 start=False, stop=True)
                    b.stage("tensor", s_w1mm,
                            reads=("q1f", ian, "b1r") + tuple(f"w1T_{i}" for i in range(8)),
                            writes=(("psY0", "psY1") if nn % 2 == 0 else ("psY2", "psY3")))

                    def s_gelu(e, nn=nn, u8=u8):
                        e.activation(out=h_act[:, nn * 512:(nn + 1) * 512],
                                     in_=psY[:, (nn % 2) * 512:(nn % 2) * 512 + 512],
                                     func=AF.Gelu, scale=al1[u8][:, 0:1])
                    b.stage("scalar", s_gelu,
                            reads=(("psY0", "psY1") if nn % 2 == 0 else ("psY2", "psY3")) + (f"al1_{u8}",),
                            writes=(f"ha{nn}",))

            def p2_back(u):
                ti, s = divmod(u, NSUB)
                r0 = ti * T
                tpar = ti % 2
                up = u % 4
                u8 = u % 8
                c2 = c2p[up]
                cn = f"c2_{up}"
                da, dan = dap[up], f"da{up}"
                db, dbn = dbp[up], f"db{up}"
                ia_row = iarp[up]
                ian = f"ia{up}"

                for qq in range(8):
                    def s_st2q(e, qq=qq):
                        e.bn_stats(out=bnst2[:, qq * 6:(qq + 1) * 6],
                                   in_=h_act[:, qq * 512:(qq + 1) * 512])
                        e.reduce_max(out=amx[:, qq:qq + 1],
                                     in_=h_act[:, qq * 512:(qq + 1) * 512],
                                     axis=AX, apply_absolute_value=True)
                    b.stage("vector", s_st2q, reads=(f"ha{qq}",),
                            writes=("bnst2", "amx"))

                def s_cs2a(e, da=da):
                    e.reduce_max(out=col(da, 0), in_=amx[:, 0:8], axis=AX)
                    e.drain()
                    e.tensor_scalar(out=col(da, 1), in0=col(da, 0),
                                    scalar1=1e-6, scalar2=None, op0=ALU.max)
                    e.drain()
                    e.reciprocal(out=col(da, 1), in_=col(da, 1))
                    e.drain()
                    e.tensor_scalar(out=col(da, 2), in0=col(da, 1),
                                    scalar1=127.0, scalar2=None, op0=ALU.mult)
                b.stage("vector", s_cs2a, reads=("amx",), writes=(dan,))

                for hh in range(8):
                    def s_q2h(e, da=da, hh=hh):
                        hs = hh * 512
                        e.tensor_scalar(out=wstage[:], in0=h_act[:, hs:hs + 512],
                                        scalar1=col(da, 2), scalar2=MAGIC,
                                        op0=ALU.mult, op1=ALU.add)
                        e.drain()
                        e.tensor_scalar(out=h_act[:, hs:hs + 512], in0=wstage[:],
                                        scalar1=MAGIC, scalar2=None,
                                        op0=ALU.subtract)
                    b.stage("vector", s_q2h, reads=(f"ha{hh}", dan),
                            writes=(f"ha{hh}", "wstage"))

                def s_cs2b(e, da=da, db=db):
                    e.bn_aggr(out=db[:, 4:6], in_=bnst2[:, 0:48])
                    e.drain()
                    e.tensor_tensor(out=col(db, 6), in0=col(db, 4),
                                    in1=col(db, 4), op=ALU.mult)
                    e.drain()
                    e.tensor_tensor(out=col(db, 6), in0=col(db, 6),
                                    in1=col(db, 5), op=ALU.add)
                    e.drain()
                    e.tensor_scalar(out=col(db, 6), in0=col(db, 6),
                                    scalar1=EPS, scalar2=None, op0=ALU.add)
                    e.drain()
                    qrsqrt(e, col(db, 2), col(db, 6), col(db, 0), col(db, 1))
                    e.tensor_tensor(out=col(db, 3), in0=col(da, 0),
                                    in1=col(db, 2), op=ALU.mult)
                    e.drain()
                    e.tensor_scalar(out=al2[u8][:, 0:1], in0=col(db, 3),
                                    scalar1=col(smallc, 2),
                                    scalar2=1.0 / 127.0,
                                    op0=ALU.mult, op1=ALU.mult)
                    e.drain()
                    e.reciprocal(out=col(db, 7), in_=al2[u8][:, 0:1])
                    e.drain()
                    e.tensor_copy(out=iacol2[up][:], in_=col(db, 7))
                b.stage("vector", s_cs2b, reads=(dan, "smallc", "bnst2"),
                        writes=(dbn, f"al2_{u8}", f"iac2{up}"))

                def s_ia2t(e, up=up):
                    e.transpose(psT2a[0:1, 0:128], iacol2[up][:], ident[:])
                b.stage("tensor", s_ia2t, reads=(f"iac2{up}", "ident"),
                        writes=("psT2a",))

                def s_ia2e(e, ia_row=ia_row):
                    e.activation(out=ia_row[0:1, :], in_=psT2a[0:1, 0:128],
                                 func=AF.Copy)
                b.stage("scalar", s_ia2e, reads=("psT2a",), writes=(ian,))

                for half in range(2):
                    for rr in range(2):
                        gb = half * 2 + rr
                        pst = psT2[gb % 2]
                        pstn = f"psT2{'ab'[gb % 2]}"

                        def s_tq2(e, half=half, rr=rr, pst=pst):
                            for j in range(8):
                                fc = half * 16 + rr * 8 + j
                                e.transpose(pst[:, j * 128:(j + 1) * 128],
                                            h_act[:, fc * 128:(fc + 1) * 128],
                                            ident[:])
                        b.stage("tensor", s_tq2,
                                reads=(f"ha{half * 4 + rr * 2}",
                                       f"ha{half * 4 + rr * 2 + 1}", "ident"),
                                writes=(pstn,))

                        def s_tq2e(e, half=half, rr=rr, pst=pst):
                            for j in range(8):
                                lc = rr * 8 + j
                                e.activation(out=q2f[lc][:],
                                             in_=pst[:, j * 128:(j + 1) * 128],
                                             func=AF.Copy)
                        b.stage("scalar", s_tq2e, reads=(pstn,), writes=("q2f",))

                    def s_w2mm(e, half=half, ia_row=ia_row):
                        for nn in range(2):
                            ph = psY[:, (2 + nn) * 512:(2 + nn) * 512 + 512]
                            for lc in range(16):
                                fc = half * 16 + lc
                                e.matmul(ph, q2f[lc][:],
                                         w2q[fc][:, nn * 512:(nn + 1) * 512],
                                         start=(fc == 0), stop=False)
                            if half == 1:
                                e.matmul(ph, ia_row[0:1, :],
                                         b2gb_r[0:1, nn * 512:(nn + 1) * 512],
                                         start=False, stop=True)
                    b.stage("tensor", s_w2mm,
                            reads=("q2f", ian, "b2gbr") +
                            tuple(f"w2T_{i}" for i in range(32)),
                            writes=("psY4", "psY5", "psY6", "psY7"))

                def s_res2(e, s=s, tpar=tpar, u8=u8):
                    for nn in range(2):
                        e.scalar_tensor_tensor(
                            out=xo_sb[s][:, nn * 512:(nn + 1) * 512],
                            in0=psY[:, (2 + nn) * 512:(2 + nn) * 512 + 512],
                            scalar=al2[u8][:, 0:1],
                            in1=x_sb[tpar][s][:, nn * 512:(nn + 1) * 512],
                            op0=ALU.mult, op1=ALU.add)
                b.stage("vector", s_res2,
                        reads=(f"x{tpar}{s}", f"al2_{u8}",
                               "psY4", "psY5", "psY6", "psY7"),
                        writes=(f"xo{s}",))

                def s_sto(e, sem, r0=r0, s=s):
                    e.dma_start(out=out_d.ap()[r0 + s * 128:r0 + (s + 1) * 128, :],
                                in_=xo_sb[s][:]).then_inc(sem, 16)
                b.dma(s_sto, 1, reads=(f"xo{s}",), writes=("outd",))

            NU = NT * NSUB
            p2_front(0)
            p2_mid(0)
            for u in range(NU):
                if u + 1 < NU:
                    p2_front(u + 1)
                p2_back(u)
                if u + 1 < NU:
                    p2_mid(u + 1)

        b.finish()

        with nc.Block() as block:
            b.emit(block)

    return nc


def _host_consts(L):
    ident = np.eye(128, dtype=np.float32)
    import ml_dtypes
    ident = ident.astype(ml_dtypes.bfloat16)
    ones_row = np.ones((1, 128), dtype=np.float32)
    sel = np.zeros((K, 8 * 128), dtype=np.float32)
    for oc in range(8):
        for p in range(128):
            sel[4 * oc + p // 32, oc * 128 + p] = 1.0
    gs = np.zeros((128, 8 * K), dtype=np.float32)
    for oc in range(8):
        for kl in range(4):
            for h in range(32):
                gs[kl * 32 + h, oc * K + 4 * oc + kl] = 1.0
    gs = gs.astype(ml_dtypes.bfloat16)
    iota = np.arange(L, dtype=np.float32)
    iotaD = np.zeros((K, L), dtype=np.float32)
    iotaD[0:28, :] = (L - 1) - iota
    iotaD[28:32, :] = iota
    return {"ident": ident, "ident_f": np.eye(128, dtype=np.float32),
            "ones_row": ones_row,
            "ones_col": np.ones((128, 1), dtype=np.float32),
            "Sel": sel, "GS": gs, "iotaD": iotaD}


def make_inputs(xrow, crow, in_proj_w, conv_w, conv_b, theta, decay, anchor,
                score_scale, deriv_logits, norm_scale, W_re, W_im, out_proj_w,
                w1, b1, w2, b2, wmod, L):
    th = np.asarray(theta).reshape(K, H, MM_)
    th_all = np.zeros((128, K), dtype=np.float32)
    for oc in range(8):
        for p in range(128):
            kl, h = p // 32, p % 32
            for m in range(MM_):
                th_all[p, oc * 4 + m] = th[4 * oc + kl, h, m]
    cw = np.asarray(conv_w).reshape(4, TOT)
    wconv = np.zeros((128, 4 * NC17), dtype=np.float32)
    cbv = np.zeros((128, NC17), dtype=np.float32)
    for oc in range(NC17):
        w = CHUNK[oc]
        for d in range(4):
            wconv[0:w, oc * 4 + d] = cw[d, oc * 128:oc * 128 + w]
        cbv[0:w, oc] = np.asarray(conv_b)[oc * 128:oc * 128 + w]
    nsBD = np.zeros((128, 8), dtype=np.float32)
    ns = np.asarray(norm_scale)
    for j in range(4):
        for h in range(32):
            p = 32 * j + h
            for m in range(MM_):
                nsBD[p, m] = ns[h * MM_ + m]
                nsBD[p, 4 + m] = ns[128 + h * MM_ + m]
    BDraw = np.zeros((128, 8 * 128), dtype=np.float32)
    Wre = np.asarray(W_re)
    Wim = np.asarray(W_im)
    for m in range(MM_):
        for j in range(4):
            for h in range(32):
                BDraw[32 * j + h, m * 128 + 32 * j:m * 128 + 32 * j + 32] = \
                    Wre[h * MM_ + m, :]
                BDraw[32 * j + h, (4 + m) * 128 + 32 * j:(4 + m) * 128 + 32 * j + 32] = \
                    Wim[h * MM_ + m, :]
    slopes = np.concatenate([np.asarray(decay).reshape(-1),
                             np.asarray(anchor).reshape(-1)]).reshape(K, 1)
    d = {
        "x": np.ascontiguousarray(xrow, dtype=np.float32),
        "in_proj_w": np.ascontiguousarray(in_proj_w, dtype=np.float32),
        "out_proj_w": np.ascontiguousarray(out_proj_w, dtype=np.float32),
        "w1T": np.ascontiguousarray(np.asarray(w1).T, dtype=np.float32),
        "w2T": np.ascontiguousarray(np.asarray(w2).T, dtype=np.float32),
        "b1": np.ascontiguousarray(np.asarray(b1).reshape(1, FF), dtype=np.float32),
        "b2": np.ascontiguousarray(np.asarray(b2).reshape(1, D), dtype=np.float32),
        "c_cols": np.ascontiguousarray(np.asarray(crow).reshape(8, 128).T,
                                       dtype=np.float32),
        "wmod": np.ascontiguousarray(wmod, dtype=np.float32),
        "wconv_cols": wconv,
        "convb_cols": cbv,
        "th_all": th_all,
        "ss_col": np.ascontiguousarray(np.asarray(score_scale).reshape(K, 1),
                                       dtype=np.float32),
        "slopes_col": np.ascontiguousarray(slopes, dtype=np.float32),
        "dl_row": np.ascontiguousarray(np.asarray(deriv_logits).reshape(1, 3),
                                       dtype=np.float32),
        "nsBD": nsBD,
        "BDraw": BDraw,
    }
    d.update(_host_consts(L))
    return d


_NC_CACHE = {}


def kernel(L0_attn_x, L0_attn_in_proj_w, L0_attn_conv_w, L0_attn_conv_b,
           L0_attn_theta, L0_attn_decay_slopes, L0_attn_anchor_slopes,
           L0_attn_score_scale, L0_attn_deriv_logits, L0_attn_norm_scale,
           L0_attn_W_re, L0_attn_W_im, L0_attn_out_proj_w,
           L0_ffn_w1, L0_ffn_b1, L0_ffn_w2, L0_ffn_b2, L0_c, L0_wmod):
    L = 4096
    if L not in _NC_CACHE:
        _NC_CACHE[L] = build_nc(L=L, T=256)
    nc = _NC_CACHE[L]
    x = np.ascontiguousarray(np.asarray(L0_attn_x), dtype=np.float32)
    in_maps = []
    for bi in range(8):
        in_maps.append(make_inputs(
            x[bi], np.asarray(L0_c)[bi], L0_attn_in_proj_w, L0_attn_conv_w,
            L0_attn_conv_b, L0_attn_theta, L0_attn_decay_slopes,
            L0_attn_anchor_slopes, L0_attn_score_scale, L0_attn_deriv_logits,
            L0_attn_norm_scale, L0_attn_W_re, L0_attn_W_im, L0_attn_out_proj_w,
            L0_ffn_w1, L0_ffn_b1, L0_ffn_w2, L0_ffn_b2, L0_wmod, L))
    res = run_bass_kernel_spmd(nc, in_maps, core_ids=list(range(8)))
    out = np.stack([res.results[bi]["out"] for bi in range(8)], axis=0)
    return out.astype(np.float32)


# version 25 (final)
# speedup vs baseline: 1.0901x; 1.0901x over previous
import contextlib
import math
import numpy as np
import concourse.bass as bass
import concourse.mybir as mybir
from concourse.bass_utils import run_bass_kernel_spmd

F32 = mybir.dt.float32
BF16 = mybir.dt.bfloat16
AF = mybir.ActivationFunctionType
ALU = mybir.AluOpType
AX = mybir.AxisListType.X

MAGIC = 1.5 * 2.0 ** 23
INV2PI = float(1.0 / (2.0 * math.pi))
TWOPI = float(2.0 * math.pi)

D = 1024
TOT = 2080
K = 32
H = 32
MM_ = 4
FF = 4096
NC17 = 17
CHUNK = [128] * 16 + [32]
EPS = 1e-5

ENGINES = ("sync", "vector", "scalar", "tensor", "gpsimd")


class _LastInst:

    def __init__(self, eng):
        self._e = eng
        self.last = None

    def __getattr__(self, n):
        a = getattr(self._e, n)
        if callable(a):
            def call(*args, _a=a, **kw):
                r = _a(*args, **kw)
                if hasattr(r, "then_inc"):
                    self.last = r
                return r
            return call
        return a


class _EstInst:
    def then_inc(self, *a, **k):
        return self


def _ap_free(ap):
    try:
        sh = ap.shape
    except Exception:
        return 256
    n = 1
    for d in sh[1:]:
        n *= d
    return max(n, 1)


def _ap_16bit(ap):
    try:
        return "16" in str(ap.dtype)
    except Exception:
        return False


class _Est:

    def __init__(self, eng):
        self.eng = eng
        self.t = 0.0

    def __getattr__(self, name):
        def call(*args, **kw):
            t = 0.0
            out = kw.get("out")
            if out is None and args:
                out = args[0]
            if name == "matmul":
                rhs = kw.get("rhs", args[2] if len(args) > 2 else None)
                n = _ap_free(rhs) if rhs is not None else 256
                t = 45 + n * 0.45
            elif name == "transpose":
                t = 140 + 128 * 0.45
            elif name in ("drain", "nop", "engine_nop"):
                t = 40
            elif name.startswith("dma"):
                t = 120
            elif name in ("wait_ge",):
                t = 20
            else:
                f = _ap_free(out) if out is not None else 128
                if self.eng == "scalar":
                    t = 190 + f * 0.85
                elif self.eng == "gpsimd":
                    t = 300 + f * 4.0
                else:
                    rate = 1.08
                    if name in ("tensor_scalar", "tensor_copy", "memset",
                                "tensor_scalar_mul", "tensor_scalar_add",
                                "tensor_relu", "copy"):
                        rate = 0.55
                    if out is not None and _ap_16bit(out):
                        rate *= 0.7
                    t = 80 + f * rate
            self.t += t
            return _EstInst()
        return call


class _Node:
    __slots__ = ("i", "eng", "body", "preds", "kind", "ndma", "est", "seg")

    def __init__(self, i, eng, body, kind, ndma, est, seg):
        self.i = i
        self.eng = eng
        self.body = body
        self.preds = set()
        self.kind = kind
        self.ndma = ndma
        self.est = est
        self.seg = seg


class B:

    def __init__(self, nc, stack):
        self.nc = nc
        self.sem = {}
        for e in ENGINES:
            self.sem[e] = stack.enter_context(nc.semaphore(f"sp_{e}"))
        self.NHD = 8
        self.NGD = 4
        self.dpool = [stack.enter_context(nc.semaphore(f"dma_h{i}"))
                      for i in range(self.NHD)]
        self.gpool = [stack.enter_context(nc.semaphore(f"dma_g{i}"))
                      for i in range(self.NGD)]
        self.nodes = []
        self.last_w = {}
        self.readers = {}
        self.seg = 0
        self.ev = {e: [] for e in ENGINES}

    def _add(self, eng, body, reads, writes, kind, ndma, est):
        n = _Node(len(self.nodes), eng, body, kind, ndma, est, self.seg)
        for bnm in list(reads) + list(writes):
            p = self.last_w.get(bnm)
            if p is not None:
                n.preds.add(p)
        for bnm in writes:
            for p in self.readers.get(bnm, ()):
                n.preds.add(p)
        n.preds.discard(n.i)
        self.nodes.append(n)
        for bnm in reads:
            self.readers.setdefault(bnm, []).append(n.i)
        for bnm in writes:
            self.last_w[bnm] = n.i
            self.readers[bnm] = []
        return n

    def stage(self, eng, body, reads=(), writes=()):
        est = _Est(eng)
        try:
            body(est)
        except Exception:
            est.t = 1000.0
        self._add(eng, body, reads, writes, "stage", 0, est.t)

    def dma(self, body, n, eng="sync", reads=(), writes=()):
        self._add(eng, body, reads, writes,
                  "dma" if eng == "sync" else "gdma", n, 2000.0 + n * 1500.0)

    def barrier(self):
        self.seg += 1

    def finish(self):
        pass

    def _schedule(self):
        nodes = self.nodes
        nseg = self.seg + 1
        order = []
        eng_t = {e: 0.0 for e in ENGINES}
        finish = [0.0] * len(nodes)
        for sg in range(nseg):
            segnodes = [n for n in nodes if n.seg == sg]
            if not segnodes:
                continue
            base = max(eng_t.values())
            for e in eng_t:
                eng_t[e] = base
            indeg = {}
            succs = {}
            segset = {n.i for n in segnodes}
            for n in segnodes:
                cnt = 0
                for p in n.preds:
                    if p in segset:
                        cnt += 1
                        succs.setdefault(p, []).append(n.i)
                indeg[n.i] = cnt
            ready = [n.i for n in segnodes if indeg[n.i] == 0]
            nmap = {n.i: n for n in segnodes}
            import heapq
            done = 0
            while ready:
                best = None
                best_key = None
                for i in ready:
                    n = nmap[i]
                    st = eng_t[n.eng]
                    for p in n.preds:
                        if p in segset and finish[p] > st:
                            st = finish[p]
                    key = (st, i)
                    if best_key is None or key < best_key:
                        best_key = key
                        best = i
                ready.remove(best)
                n = nmap[best]
                st = best_key[0]
                if n.kind == "stage":
                    finish[best] = st + n.est
                    eng_t[n.eng] = st + n.est
                else:
                    eng_t[n.eng] = st + 300.0
                    finish[best] = st + n.est
                order.append(best)
                for s2 in succs.get(best, ()):
                    indeg[s2] -= 1
                    if indeg[s2] == 0:
                        ready.append(s2)
            assert len(order) == sum(1 for n in nodes if n.seg <= sg), "cycle"
        return order

    def _emit_all(self):
        order = self._schedule()
        nodes = self.nodes
        seqno = [0] * len(nodes)
        cnt = {e: 0 for e in ENGINES}
        dseq = [None] * len(nodes)
        dpc = [0] * self.NHD
        gpc = [0] * self.NGD
        drr = grr = 0
        for i in order:
            n = nodes[i]
            if n.kind == "stage":
                cnt[n.eng] += 1
                seqno[i] = cnt[n.eng]
            elif n.kind == "dma":
                k = drr % self.NHD
                drr += 1
                dseq[i] = ("hd", k, dpc[k], dpc[k] + 16 * n.ndma)
                dpc[k] += 16 * n.ndma
            else:
                k = grr % self.NGD
                grr += 1
                dseq[i] = ("gd", k, gpc[k], gpc[k] + 16 * n.ndma)
                gpc[k] += 16 * n.ndma

        def tok(i):
            n = nodes[i]
            if n.kind == "stage":
                return (n.eng, seqno[i])
            kd, k, prev, val = dseq[i]
            return (f"{kd}{k}", val)

        def semof(kind):
            if kind.startswith("hd"):
                return self.dpool[int(kind[2:])]
            if kind.startswith("gd"):
                return self.gpool[int(kind[2:])]
            return self.sem[kind]

        known = {e: {} for e in ENGINES}
        running = {}
        cur_seg = 0
        for i in order:
            n = nodes[i]
            if n.seg != cur_seg:
                for e in ENGINES:
                    kn = known[e]
                    for kd, v in running.items():
                        if kd == e or kn.get(kd, 0) >= v:
                            continue
                        self.ev[e].append(
                            lambda en, s=semof(kd), v=v: en.wait_ge(s, v))
                        kn[kd] = v
                cur_seg = n.seg
            eng = n.eng
            kn = known[eng]
            need = {}
            for p in n.preds:
                kd, v = tok(p)
                if kd == eng:
                    continue
                if need.get(kd, 0) < v:
                    need[kd] = v
            pre = []
            for kd in sorted(need):
                v = need[kd]
                if kn.get(kd, 0) >= v:
                    continue
                pre.append((semof(kd), v))
                kn[kd] = v
            if n.kind == "stage":
                def wrapped(e, pre=pre, body=n.body, eng=eng,
                            s=self.sem[eng]):
                    for sm, v in pre:
                        e.wait_ge(sm, v)
                    if eng == "tensor":
                        prox = _LastInst(e)
                        body(prox)
                        assert prox.last is not None
                        prox.last.then_inc(s, 1)
                    else:
                        body(e)
                        e.drain().then_inc(s, 1)
                self.ev[eng].append(wrapped)
                kd, v = tok(i)
                running[kd] = v
            else:
                kd, k, prev, val = dseq[i]
                kdk = f"{kd}{k}"
                sm = semof(kdk)
                if prev and kn.get(kdk, 0) < prev:
                    pre.append((sm, prev))
                    kn[kdk] = prev
                def wrappedd(e, pre=pre, body=n.body, sm=sm):
                    for s2, v in pre:
                        e.wait_ge(s2, v)
                    body(e, sm)
                self.ev[eng].append(wrappedd)
                running[kdk] = val
        for k, c in enumerate(dpc):
            if c:
                self.ev["sync"].append(
                    lambda e, s=self.dpool[k], v=c: e.wait_ge(s, v))
        for k, c in enumerate(gpc):
            if c:
                self.ev["sync"].append(
                    lambda e, s=self.gpool[k], v=c: e.wait_ge(s, v))

    def emit(self, block):
        self._emit_all()
        for name in ENGINES:
            evs = self.ev[name]

            def section(eng, evs=evs):
                for fn in evs:
                    fn(eng)

            getattr(block, name)(section)


def col(t, j):
    return t[:, j:j + 1]


def build_nc(L=4096, T=256):
    NT = L // T
    NSUB = T // 128
    nc = bass.Bass("TRN2", target_bir_lowering=False)

    dram = {}

    def din(name, shape, dtype=F32):
        dram[name] = nc.dram_tensor(name, list(shape), dtype, kind="ExternalInput")
        return dram[name]

    din("x", (L, D))
    din("in_proj_w", (D, TOT))
    din("out_proj_w", (D, D))
    din("w1T", (D, FF))
    din("w2T", (FF, D))
    din("b1", (1, FF))
    din("b2", (1, D))
    din("c_cols", (128, 8))
    din("wmod", (D, 6 * D))
    din("wconv_cols", (128, 4 * NC17))
    din("convb_cols", (128, NC17))
    din("th_all", (128, K))
    din("ss_col", (K, 1))
    din("slopes_col", (K, 1))
    din("dl_row", (1, 3))
    din("nsBD", (128, 8))
    din("BDraw", (128, 8 * 128))
    din("ident", (128, 128), BF16)
    din("ident_f", (128, 128))
    din("ones_row", (1, 128))
    din("ones_col", (128, 1))
    din("Sel", (K, 8 * 128))
    din("GS", (128, 8 * K), BF16)
    din("iotaD", (K, L))

    out_d = nc.dram_tensor("out", [L, D], F32, kind="ExternalOutput")
    xmid = nc.dram_tensor("xmid", [L, D], F32, kind="Internal")
    logtwd = nc.dram_tensor("logtwd", [K, L], F32, kind="Internal")

    ap = {k: v.ap() for k, v in dram.items()}

    with contextlib.ExitStack() as S0:
        b = B(nc, S0)

        def sb(name, shape, dtype=F32, stack=S0):
            return stack.enter_context(nc.sbuf_tensor(name, list(shape), dtype))

        ident = sb("identb", (128, 128), BF16)
        ident_f = sb("identf", (128, 128))
        ones_r = sb("ones_r", (1, 128))
        ones_c = sb("ones_c", (128, 1))
        gs_sb = sb("gs_sb", (128, 8 * K), BF16)
        bd_sb = sb("bd_sb", (128, 8 * 128), BF16)
        th_sb = sb("th_sb", (128, K))
        wcv_sb = sb("wcv_sb", (128, 4 * NC17))
        cbv_sb = sb("cbv_sb", (128, NC17))
        ss_sb = sb("ss_sb", (K, 1))
        slp_sb = sb("slp_sb", (K, 1))
        dl_sb = sb("dl_sb", (1, 3))
        w012 = sb("w012", (128, 3))
        sa_c = sb("sa_c", (128, 8))
        ca1_c = sb("ca1_c", (128, 8))

        cb1B = sb("cb1B", (128, D))
        gaB = sb("gaB", (128, D))
        sbB = sb("sbB", (128, D))
        gbB = sb("gbB", (128, D))

        b2gb_r = sb("b2gb_r", (1, D), BF16)
        b1_r = sb("b1_r", (1, FF), BF16)
        x_sb = [[sb(f"x_sb{p}{s}", (128, D)) for s in range(NSUB)] for p in range(2)]
        xo_sb = [sb(f"xo_sb{s}", (128, D)) for s in range(NSUB)]
        smallc = sb("smallc", (128, 16))
        srow = sb("srow", (1, 16))

        def ld_consts(e, sem):
            for name, dst in [("ident", ident), ("ident_f", ident_f),
                              ("ones_row", ones_r), ("ones_col", ones_c),
                              ("GS", gs_sb), ("th_all", th_sb),
                              ("wconv_cols", wcv_sb), ("convb_cols", cbv_sb),
                              ("ss_col", ss_sb), ("slopes_col", slp_sb),
                              ("dl_row", dl_sb)]:
                e.dma_start(out=dst[:], in_=ap[name]).then_inc(sem, 16)
        b.dma(ld_consts, 11, reads=("SU",), writes=("SU", "ident", "identf", "ones", "gs", "th", "wcv", "cbv", "ss", "slp", "dl"))

        def s_theta(e):
            e.tensor_scalar(out=th_sb[:], in0=th_sb[:], scalar1=INV2PI,
                            scalar2=None, op0=ALU.mult)
        b.stage("vector", s_theta, reads=("SU", "th"), writes=("SU", "th"))

        def s_sp1(e):
            e.activation(out=smallc[0:K, 0:1], in_=slp_sb[:], func=AF.Abs)
        b.stage("scalar", s_sp1, reads=("SU",), writes=("SU",))

        def s_sp2(e):
            e.activation(out=smallc[0:K, 1:2], in_=smallc[0:K, 0:1], func=AF.Exp,
                         scale=-1.0)
        b.stage("scalar", s_sp2, reads=("SU",), writes=("SU",))

        def s_sp3(e):
            e.tensor_scalar(out=smallc[0:K, 2:3], in0=smallc[0:K, 1:2],
                            scalar1=1.0, scalar2=None, op0=ALU.add)
        b.stage("vector", s_sp3, reads=("SU",), writes=("SU",))

        def s_sp4(e):
            e.activation(out=smallc[0:K, 3:4], in_=smallc[0:K, 2:3], func=AF.Ln)
        b.stage("scalar", s_sp4, reads=("SU",), writes=("SU",))

        def s_sp5(e):
            e.tensor_relu(out=smallc[0:K, 4:5], in_=slp_sb[:])
            e.drain()
            e.tensor_tensor(out=smallc[0:K, 5:6], in0=smallc[0:K, 4:5],
                            in1=smallc[0:K, 3:4], op=ALU.add)
            e.drain()
            e.tensor_scalar(out=smallc[0:K, 6:7], in0=smallc[0:K, 5:6],
                            scalar1=-1.0, scalar2=None, op0=ALU.mult)
        b.stage("vector", s_sp5, reads=("SU",), writes=("SU", "smallc"))

        def s_dl1(e):
            e.reduce_max(out=srow[0:1, 0:1], in_=dl_sb[:], axis=AX)
            e.drain()
            e.tensor_scalar(out=srow[0:1, 1:4], in0=dl_sb[:],
                            scalar1=srow[0:1, 0:1], scalar2=None, op0=ALU.subtract)
        b.stage("vector", s_dl1, reads=("SU",), writes=("SU",))

        def s_dl2(e):
            e.activation(out=srow[0:1, 4:7], in_=srow[0:1, 1:4], func=AF.Exp)
        b.stage("scalar", s_dl2, reads=("SU",), writes=("SU",))

        def s_dl3(e):
            e.reduce_sum(out=srow[0:1, 7:8], in_=srow[0:1, 4:7], axis=AX)
            e.drain()
            e.reciprocal(out=srow[0:1, 8:9], in_=srow[0:1, 7:8])
            e.drain()
            e.tensor_scalar(out=srow[0:1, 9:12], in0=srow[0:1, 4:7],
                            scalar1=srow[0:1, 8:9], scalar2=None, op0=ALU.mult)
        b.stage("vector", s_dl3, reads=("SU",), writes=("SU", "srow"))

        StRows = contextlib.ExitStack()
        row_ga = sb("row_ga", (1, D), F32, StRows)
        row_sb = sb("row_sb", (1, D), F32, StRows)
        row_cb1 = sb("row_cb1", (1, D), F32, StRows)
        row_gb = sb("row_gb", (1, D), F32, StRows)
        rscr = sb("rscr", (1, 2 * D), F32, StRows)
        with contextlib.ExitStack() as St0:
            psW = St0.enter_context(nc.psum_tensor("psW", [128, 16], F32))
            psMC = St0.enter_context(nc.psum_tensor("psMC", [128, 48], F32))
            psTr = St0.enter_context(nc.psum_tensor("psTr", [1, 128], F32))
            wmstg = [St0.enter_context(nc.sbuf_tensor(f"wmstg{i}", [128, 6 * D], F32))
                     for i in range(2)]
            siluc = St0.enter_context(nc.sbuf_tensor("siluc", [128, 8], F32))
            cvec = St0.enter_context(nc.sbuf_tensor("cvec", [128, 8], F32))
            mcols = St0.enter_context(nc.sbuf_tensor("mcols", [128, 48], F32))
            b.stage("vector", lambda e: e.memset(mcols[:], 0.0), reads=("SU",), writes=("SU",))

            def s_w012mm(e):
                e.matmul(psW[:, 0:3], ones_r[:], srow[0:1, 9:12],
                         start=True, stop=True)
            b.stage("tensor", s_w012mm, reads=("SU", "srow", "ones"), writes=("SU",))

            def s_w012ev(e):
                e.tensor_copy(out=w012[:], in_=psW[:, 0:3])
                e.drain()
                e.tensor_scalar(out=w012[:, 2:3], in0=w012[:, 2:3],
                                scalar1=-1.0, scalar2=None, op0=ALU.mult)
            b.stage("vector", s_w012ev, reads=("SU",), writes=("SU", "w012"))

            b.dma(lambda e, sem: e.dma_start(out=cvec[:], in_=ap["c_cols"])
                  .then_inc(sem, 16), 1, reads=("SU",), writes=("SU",))
            b.stage("scalar", lambda e: e.activation(out=siluc[:], in_=cvec[:],
                                                     func=AF.Silu),
                    reads=("SU",), writes=("SU",))

            for fc in range(8):
                stg = wmstg[fc % 2]
                b.dma(lambda e, sem, fc=fc, stg=stg:
                      e.dma_start(out=stg[:], in_=ap["wmod"][fc * 128:(fc + 1) * 128, :])
                      .then_inc(sem, 16), 1, reads=("SU",), writes=("SU",))

                def s_modmm(e, fc=fc, stg=stg):
                    for oc in range(48):
                        e.matmul(psMC[:, oc:oc + 1], stg[:, oc * 128:(oc + 1) * 128],
                                 siluc[:, fc:fc + 1], start=True, stop=True)
                b.stage("tensor", s_modmm, reads=("SU",), writes=("SU",))

                def s_macc(e):
                    e.tensor_tensor(out=mcols[:], in0=mcols[:], in1=psMC[:],
                                    op=ALU.add)
                b.stage("vector", s_macc, reads=("SU",), writes=("SU",))

            def s_modev(e):
                e.tensor_copy(out=sa_c[:], in_=mcols[:, 0:8])
                e.tensor_scalar(out=ca1_c[:], in0=mcols[:, 8:16], scalar1=1.0,
                                scalar2=None, op0=ALU.add)
            b.stage("vector", s_modev, reads=("SU",), writes=("SU", "sa", "ca1"))

            for gi, dst in ((0, row_ga), (1, row_sb), (2, row_cb1), (3, row_gb)):
                for j in range(8):
                    def s_rt(e, gi=gi, j=j):
                        e.transpose(psTr[:, :], mcols[:, 16 + gi * 8 + j:17 + gi * 8 + j],
                                    ident_f[:])
                    b.stage("tensor", s_rt, reads=("SU", "identf"), writes=("SU",))

                    def s_rte(e, dst=dst, j=j):
                        e.tensor_copy(out=dst[0:1, j * 128:(j + 1) * 128],
                                      in_=psTr[0:1, :])
                    b.stage("vector", s_rte, reads=("SU",), writes=("SU",))

            for rowt in (row_ga, row_gb):
                def s_gabs(e, rowt=rowt):
                    e.activation(out=rscr[0:1, 0:D], in_=rowt[0:1, :], func=AF.Abs)
                b.stage("scalar", s_gabs, reads=("SU",), writes=("SU",))

                def s_gfix(e, rowt=rowt):
                    e.tensor_scalar(out=rscr[0:1, D:2 * D], in0=rowt[0:1, :],
                                    scalar1=0.0, scalar2=None, op0=ALU.is_ge)
                    e.tensor_scalar(out=rscr[0:1, 0:D], in0=rscr[0:1, 0:D],
                                    scalar1=0.25, scalar2=None, op0=ALU.max)
                    e.drain()
                    e.tensor_scalar(out=rscr[0:1, D:2 * D], in0=rscr[0:1, D:2 * D],
                                    scalar1=2.0, scalar2=-1.0,
                                    op0=ALU.mult, op1=ALU.add)
                    e.drain()
                    e.tensor_tensor(out=rowt[0:1, :], in0=rscr[0:1, D:2 * D],
                                    in1=rscr[0:1, 0:D], op=ALU.mult)
                b.stage("vector", s_gfix, reads=("SU",), writes=("SU",))

            def s_cb1(e):
                e.tensor_scalar(out=row_cb1[0:1, :], in0=row_cb1[0:1, :],
                                scalar1=1.0, scalar2=None, op0=ALU.add)
            b.stage("vector", s_cb1, reads=("SU",), writes=("SU",))

        with contextlib.ExitStack() as St1:
            psB = St1.enter_context(nc.psum_tensor("psB", [128, D], F32))
            bdstg = St1.enter_context(nc.sbuf_tensor("bdstg", [128, D], F32))
            nsbd = St1.enter_context(nc.sbuf_tensor("nsbd", [128, 8], F32))
            b2row = St1.enter_context(nc.sbuf_tensor("b2row", [1, D], F32))
            b2gbf = St1.enter_context(nc.sbuf_tensor("b2gbf", [1, D], F32))

            for rowt, dst in ((row_ga, gaB), (row_sb, sbB),
                              (row_cb1, cb1B), (row_gb, gbB)):
                def s_bmm(e, rowt=rowt):
                    e.matmul(psB[:, 0:512], ones_r[:], rowt[0:1, 0:512],
                             start=True, stop=True)
                    e.matmul(psB[:, 512:1024], ones_r[:], rowt[0:1, 512:1024],
                             start=True, stop=True)
                b.stage("tensor", s_bmm, reads=("SU", "ones"), writes=("SU",))

                def s_bev(e, dst=dst):
                    e.tensor_copy(out=dst[:], in_=psB[:])
                b.stage("vector", s_bev, reads=("SU",), writes=("SU", "gaB", "sbB", "cb1B", "gbB"))

            b.dma(lambda e, sem: e.dma_start(out=b2row[:], in_=ap["b2"])
                  .then_inc(sem, 16), 1, reads=("SU",), writes=("SU",))

            def s_b2gb(e):
                e.tensor_tensor(out=b2gbf[:], in0=b2row[:], in1=row_gb[0:1, :],
                                op=ALU.mult)
                e.drain()
                e.tensor_copy(out=b2gb_r[:], in_=b2gbf[:])
            b.stage("vector", s_b2gb, reads=("SU",), writes=("SU", "b2gbr"))

            b.dma(lambda e, sem: e.dma_start(out=b1_r[:], in_=ap["b1"])
                  .then_inc(sem, 16), 1, eng="gpsimd", reads=("SU",), writes=("SU", "b1r"))

            b.dma(lambda e, sem: e.dma_start(out=nsbd[:], in_=ap["nsBD"])
                  .then_inc(sem, 16), 1, reads=("SU",), writes=("SU",))
            for blk in range(8):
                b.dma(lambda e, sem, blk=blk:
                      e.dma_start(out=bdstg[:, 0:128],
                                  in_=ap["BDraw"][:, blk * 128:(blk + 1) * 128])
                      .then_inc(sem, 16), 1, reads=("SU",), writes=("SU",))

                def s_bd(e, blk=blk):
                    e.tensor_scalar(out=bd_sb[:, blk * 128:(blk + 1) * 128],
                                    in0=bdstg[:, 0:128], scalar1=col(nsbd, blk),
                                    scalar2=None, op0=ALU.mult)
                b.stage("vector", s_bd, reads=("SU",), writes=("SU", "bd"))

        StRows.close()
        b.barrier()

        with contextlib.ExitStack() as SA:
            logtw_t = [sb(f"logtw{i}", (K, T), F32, SA) for i in range(2)]
            sel_sb = sb("sel_sb", (K, 8 * 128), F32, SA)
            win_bf = [sb(f"win{i}", (128, TOT), BF16, SA) for i in range(8)]
            wout_bf = [sb(f"wout{i}", (128, D), BF16, SA) for i in range(8)]
            psT = S0.enter_context(nc.psum_tensor("psT", [128, 1024], BF16))
            psY = S0.enter_context(nc.psum_tensor("psY", [128, 2048], F32))
            psQ = S0.enter_context(nc.psum_tensor("psQT", [128, 256], F32))
            psZ = SA.enter_context(nc.psum_tensor("psZ", [128, 1024], F32))
            psRp = [psZ[:, 0:256], psZ[:, 512:768]]

            b.dma(lambda e, sem: e.dma_start(out=sel_sb[:], in_=ap["Sel"])
                  .then_inc(sem, 16), 1, reads=("SU",), writes=("SU", "sel"))
            if True:
                SWA = contextlib.ExitStack()
                wstA = sb("wstA", (128, D), F32, SWA)
                lt_stg = sb("lt_stg", (K, 1024), F32, SWA)
                for lc in range(L // 1024 if L >= 1024 else 1):
                    lw = min(1024, L)
                    b.dma(lambda e, sem, lc=lc, lw=lw:
                          e.dma_start(out=lt_stg[:, 0:lw],
                                      in_=ap["iotaD"][:, lc * 1024:lc * 1024 + lw])
                          .then_inc(sem, 16), 1, reads=("SU",), writes=("SU",))

                    def s_lt(e, lc=lc, lw=lw):
                        e.tensor_scalar(out=lt_stg[:, 0:lw], in0=lt_stg[:, 0:lw],
                                        scalar1=smallc[0:K, 6:7], scalar2=None,
                                        op0=ALU.mult)
                    b.stage("vector", s_lt, reads=("SU", "smallc"), writes=("SU",))
                    b.dma(lambda e, sem, lc=lc, lw=lw:
                          e.dma_start(out=logtwd.ap()[:, lc * 1024:lc * 1024 + lw],
                                      in_=lt_stg[:, 0:lw]).then_inc(sem, 16),
                          1, reads=("SU",), writes=("SU",))

            for fc in range(8):
                b.dma(lambda e, sem, fc=fc:
                      e.dma_start(out=win_bf[fc][:],
                                  in_=ap["in_proj_w"][fc * 128:(fc + 1) * 128, :])
                      .then_inc(sem, 16), 1, eng="gpsimd", writes=("win",))
            for fc in range(8):
                b.dma(lambda e, sem, fc=fc:
                      e.dma_start(out=wstA[:],
                                  in_=ap["out_proj_w"][fc * 128:(fc + 1) * 128, :])
                      .then_inc(sem, 16), 1, reads=("wstA",), writes=("wstA",))

                def s_wo(e, fc=fc):
                    e.tensor_tensor(out=wout_bf[fc][:], in0=wstA[:], in1=gaB[:],
                                    op=ALU.mult)
                b.stage("vector", s_wo, reads=("wstA", "gaB"), writes=("wstA", "wout"))

            SWA.close()
            b.barrier()

            h_bf = [sb(f"hbf{i}", (128, T), BF16, SA) for i in range(8)]
            z_sb = [sb(f"z{i}", (128, 3 + T), F32, SA) for i in range(NC17)]
            z2_sb = [sb(f"zz{i}", (128, T), F32, SA) for i in range(NC17)]
            pw_sb = sb("pw_sb", (K, T), F32, SA)
            cpw_sb = sb("cpw_sb", (K, T), F32, SA)
            dcarry = sb("dcarry", (K, 1), F32, SA)
            dD_sb = sb("dD_sb", (K, T), F32, SA)
            d2s_sb = dD_sb
            sarg_sb = dD_sb
            sS_sb = dD_sb
            gate_bf = [sb(f"gate{i}", (128, T), BF16, SA) for i in range(8)]
            _pb = sb("polyb", (128, T), F32, SA)
            polyb = [_pb, _pb]
            g_bf = [sb(f"g_bf{i}", (128, T), BF16, SA) for i in range(2)]
            _g2 = sb("gn2", (128, MM_ * T), F32, SA)
            gn2p = [_g2, _g2]
            fbuf = [sb(f"fbuf{i}", (128, MM_ * T), F32, SA) for i in range(2)]
            nbuf = [sb(f"nbuf{i}", (128, MM_ * T), F32, SA) for i in range(2)]
            f25 = [sb(f"f25_{i}", (128, MM_ * T), F32, SA) for i in range(2)]
            sin_bf = [sb(f"sin_bf{i}", (128, MM_ * T), BF16, SA) for i in range(2)]
            cos_bf = [sb(f"cos_bf{i}", (128, MM_ * T), BF16, SA) for i in range(2)]
            are_bf = [sb(f"are_bf{i}", (128, MM_ * T), BF16, SA) for i in range(2)]
            aim_bf = [sb(f"aim_bf{i}", (128, MM_ * T), BF16, SA) for i in range(2)]
            ncar_re = sb("ncar_re", (128, K), F32, SA)
            ncar_im = sb("ncar_im", (128, K), F32, SA)
            _yc = sb("ycp_bf", (128, T), BF16, SA)
            _sr = sb("srep_bf", (128, T), BF16, SA)
            ycp_bf = [_yc, _yc]
            srep_bf = [_sr, _sr]
            lnc = sb("lnc", (128, 12), F32, SA)
            xn_bf = [sb(f"xnbf{s}", (128, D), BF16, SA) for s in range(NSUB)]


            def s_zero(e):
                for ocx in range(NC17):
                    e.memset(z_sb[ocx][:, 0:3], 0.0)
                e.memset(ncar_re[:], 0.0)
                e.memset(ncar_im[:], 0.0)
                e.memset(dcarry[:], 0.0)
            b.stage("vector", s_zero, writes=tuple(f"z{i}" for i in range(NC17)) + ("ncar", "dcarry"))

            b.barrier()

            def emit_headA(ti):
                r0 = ti * T
                tpar = ti % 2

                def s_ldx(e, sem, r0=r0, tpar=tpar):
                    for s in range(NSUB):
                        e.dma_start(out=x_sb[tpar][s][:],
                                    in_=ap["x"][r0 + s * 128:r0 + (s + 1) * 128, :]) \
                            .then_inc(sem, 16)
                b.dma(s_ldx, NSUB, writes=(f"x{tpar}0", f"x{tpar}1"))
                b.dma(lambda e, sem, r0=r0, tpar=tpar:
                      e.dma_start(out=logtw_t[tpar][:],
                                  in_=logtwd.ap()[:, r0:r0 + T]).then_inc(sem, 16),
                      1, writes=(f"logtw{tpar}",))

                def s_ln1(e, tpar=tpar):
                    for s in range(NSUB):
                        e.reduce_sum(out=col(lnc, s * 6 + 0), in_=x_sb[tpar][s][:], axis=AX)
                        e.scalar_tensor_tensor(out=(fbuf[0] if s == 0 else nbuf[0])[:, 0:D],
                                               in0=x_sb[tpar][s][:],
                                               scalar=1.0, in1=x_sb[tpar][s][:],
                                               op0=ALU.mult, op1=ALU.mult,
                                               accum_out=col(lnc, s * 6 + 1))
                    e.drain()
                    for s in range(NSUB):
                        e.tensor_scalar(out=col(lnc, s * 6 + 2),
                                        in0=col(lnc, s * 6 + 0),
                                        scalar1=1.0 / D, scalar2=None, op0=ALU.mult)
                    e.drain()
                    for s in range(NSUB):
                        e.tensor_tensor(out=col(lnc, s * 6 + 3),
                                        in0=col(lnc, s * 6 + 2),
                                        in1=col(lnc, s * 6 + 2), op=ALU.mult)
                    e.drain()
                    for s in range(NSUB):
                        e.scalar_tensor_tensor(out=col(lnc, s * 6 + 4),
                                               in0=col(lnc, s * 6 + 1),
                                               scalar=1.0 / D,
                                               in1=col(lnc, s * 6 + 3),
                                               op0=ALU.mult, op1=ALU.subtract)
                    e.drain()
                    for s in range(NSUB):
                        e.tensor_scalar(out=col(lnc, s * 6 + 4),
                                        in0=col(lnc, s * 6 + 4),
                                        scalar1=EPS, scalar2=None, op0=ALU.add)
                b.stage("vector", s_ln1, reads=(f"x{tpar}0", f"x{tpar}1"), writes=("lnc", "f0", "n0"))

                def s_rstd1(e):
                    for s in range(NSUB):
                        e.activation(out=col(lnc, s * 6 + 5),
                                     in_=col(lnc, s * 6 + 4), func=AF.Ln)
                    e.drain()
                    for s in range(NSUB):
                        e.activation(out=col(lnc, s * 6 + 5),
                                     in_=col(lnc, s * 6 + 5), func=AF.Exp,
                                     scale=-0.5)
                b.stage("scalar", s_rstd1, reads=("lnc",), writes=("lnc",))

                def s_xn(e, tpar=tpar):
                    for s in range(NSUB):
                        e.tensor_scalar(out=xn_bf[s][:], in0=x_sb[tpar][s][:],
                                        scalar1=col(lnc, s * 6 + 2),
                                        scalar2=col(lnc, s * 6 + 5),
                                        op0=ALU.subtract, op1=ALU.mult)
                b.stage("vector", s_xn, reads=(f"x{tpar}0", f"x{tpar}1", "lnc"), writes=("xn",))


            def emit_headB(ti):
                r0 = ti * T
                tpar = ti % 2
                for hh in range(2):
                    def s_tp1(e, hh=hh):
                        for j in range(4):
                            ocx = hh * 4 + j
                            for s in range(NSUB):
                                e.transpose(psT[:, (j * NSUB + s) * 128:
                                                (j * NSUB + s + 1) * 128],
                                            xn_bf[s][:, ocx * 128:(ocx + 1) * 128],
                                            ident[:])
                    b.stage("tensor", s_tp1, reads=("xn", "ident"), writes=("psT",))

                    def s_hfm(e, hh=hh):
                        for j in range(4):
                            ocx = hh * 4 + j
                            e.tensor_scalar(out=h_bf[ocx][:],
                                            in0=psT[:, j * NSUB * 128:
                                                    (j + 1) * NSUB * 128],
                                            scalar1=col(ca1_c, ocx),
                                            scalar2=col(sa_c, ocx),
                                            op0=ALU.mult, op1=ALU.add)
                    b.stage("vector", s_hfm, reads=("psT", "sa", "ca1"),
                            writes=("hbf",))

                for pz in range(9):
                    chunks = [c for c in (2 * pz, 2 * pz + 1) if c < NC17]

                    def s_zmm(e, chunks=chunks):
                        for ocz in chunks:
                            w = CHUNK[ocz]
                            pzl = psZ[0:w, (ocz % 2) * 512:(ocz % 2) * 512 + T]
                            for fc in range(8):
                                e.matmul(pzl,
                                         win_bf[fc][:, ocz * 128:ocz * 128 + w],
                                         h_bf[fc][:], start=(fc == 0),
                                         stop=(fc == 7))
                    b.stage("tensor", s_zmm, reads=("hbf", "win"),
                            writes=("psZa", "psZb"))

                    def s_zev(e, chunks=chunks):
                        for ocz in chunks:
                            w = CHUNK[ocz]
                            e.copy(out=z_sb[ocz][0:w, 3:3 + T],
                                   in_=psZ[0:w, (ocz % 2) * 512:
                                           (ocz % 2) * 512 + T])
                    b.stage("scalar", s_zev, reads=("psZa", "psZb"),
                            writes=tuple(f"z{c}" for c in chunks))

                def s_cv0(e):
                    for ocz in range(NC17):
                        w = CHUNK[ocz]
                        e.tensor_scalar(out=z2_sb[ocz][0:w, :],
                                        in0=z_sb[ocz][0:w, 0:T],
                                        scalar1=wcv_sb[0:w, ocz * 4:ocz * 4 + 1],
                                        scalar2=cbv_sb[0:w, ocz:ocz + 1],
                                        op0=ALU.mult, op1=ALU.add)
                b.stage("vector", s_cv0, reads=tuple(f"z{i}" for i in range(NC17)) + ("wcv", "cbv"), writes=("zz",))
                for dd in range(1, 4):
                    def s_cvd(e, dd=dd):
                        for ocz in range(NC17):
                            w = CHUNK[ocz]
                            e.scalar_tensor_tensor(
                                out=z2_sb[ocz][0:w, :],
                                in0=z_sb[ocz][0:w, dd:dd + T],
                                scalar=wcv_sb[0:w, ocz * 4 + dd:ocz * 4 + dd + 1],
                                in1=z2_sb[ocz][0:w, :],
                                op0=ALU.mult, op1=ALU.add)
                    b.stage("vector", s_cvd, reads=tuple(f"z{i}" for i in range(NC17)) + ("wcv", "zz"), writes=("zz",))

                def s_hist(e):
                    for ocz in range(NC17):
                        w = CHUNK[ocz]
                        e.tensor_copy(out=z_sb[ocz][0:w, 0:3],
                                      in_=z_sb[ocz][0:w, T:T + 3])
                b.stage("vector", s_hist, reads=tuple(f"z{i}" for i in range(NC17)), writes=tuple(f"z{i}" for i in range(NC17)))

                def s_sc1(e, tpar=tpar):
                    sr = z2_sb[16][0:K, :]
                    e.tensor_scalar(out=pw_sb[:], in0=sr, scalar1=ss_sb[:, 0:1],
                                    scalar2=None, op0=ALU.mult)
                    e.drain()
                    e.tensor_scalar(out=pw_sb[:], in0=pw_sb[:], scalar1=20.0,
                                    scalar2=-20.0, op0=ALU.min, op1=ALU.max)
                    e.drain()
                    e.tensor_tensor(out=pw_sb[:], in0=pw_sb[:],
                                    in1=logtw_t[tpar][:], op=ALU.add)
                b.stage("vector", s_sc1, reads=("zz", f"logtw{tpar}", "ss"), writes=("pw",))

                b.stage("scalar", lambda e: e.activation(out=pw_sb[:], in_=pw_sb[:],
                                                         func=AF.Exp),
                        reads=("pw",), writes=("pw",))

                def s_den(e):
                    e.tensor_tensor_scan(out=cpw_sb[:], data0=pw_sb[:],
                                         data1=pw_sb[:], initial=dcarry[:, 0:1],
                                         op0=ALU.add, op1=ALU.bypass)
                    e.drain()
                    e.tensor_copy(out=dcarry[:, 0:1], in_=cpw_sb[:, T - 1:T])
                    e.tensor_scalar(out=dD_sb[:], in0=cpw_sb[:], scalar1=1e-4,
                                    scalar2=None, op0=ALU.max)
                b.stage("vector", s_den, reads=("pw", "dcarry"), writes=("cpw", "dcarry", "dDx"))

                def s_d2s(e):
                    e.activation(out=d2s_sb[:], in_=dD_sb[:], func=AF.Square,
                                 scale=float(math.sqrt(1e-5)))
                    for ocx in range(8):
                        e.activation(out=gate_bf[ocx][:], in_=z2_sb[8 + ocx][:],
                                     func=AF.Silu)
                b.stage("scalar", s_d2s, reads=("dDx", "zz"), writes=("dDx", "gate"))


            def emit_tail(ti):
                r0 = ti * T
                tpar = ti % 2
                def g_pre(oc):
                    P_ = oc % 2
                    psZn = "psZa" if P_ == 0 else "psZb"
                    def s_pwrep(e, oc=oc, P_=P_):
                        e.matmul(psRp[P_][:, 0:T],
                                 sel_sb[:, oc * 128:(oc + 1) * 128],
                                 pw_sb[:], start=True, stop=True)
                    b.stage("tensor", s_pwrep, reads=("sel", "pw"), writes=(psZn,))

                    def s_poly(e, oc=oc, P_=P_):
                        e.tensor_scalar(out=polyb[P_][:], in0=z2_sb[oc][:],
                                        scalar1=col(w012, 2), scalar2=col(w012, 1),
                                        op0=ALU.mult, op1=ALU.add)
                        e.drain()
                        e.tensor_tensor(out=polyb[P_][:], in0=z2_sb[oc][:],
                                        in1=polyb[P_][:], op=ALU.mult)
                        e.drain()
                        e.scalar_tensor_tensor(out=g_bf[P_][:], in0=polyb[P_][:],
                                               scalar=col(w012, 0),
                                               in1=psRp[P_][:, 0:T],
                                               op0=ALU.add, op1=ALU.mult)
                    b.stage("vector", s_poly,
                            reads=("zz", psZn, "w012"),
                            writes=("poly0", f"g{P_}"))

                    def s_fexp(e, oc=oc, P_=P_):
                        for m in range(MM_):
                            e.activation(out=fbuf[P_][:, m * T:(m + 1) * T],
                                         in_=z2_sb[oc][:], func=AF.Copy,
                                         scale=th_sb[:, oc * 4 + m:oc * 4 + m + 1])
                    b.stage("scalar", s_fexp, reads=("zz", "th"),
                            writes=(f"f{P_}",))

                    def s_magic(e, P_=P_):
                        e.activation(out=nbuf[P_][:], in_=fbuf[P_][:], func=AF.Copy,
                                     bias=MAGIC)
                        e.drain()
                        e.activation(out=nbuf[P_][:], in_=nbuf[P_][:], func=AF.Copy,
                                     bias=-MAGIC)
                        e.activation(out=f25[P_][:], in_=fbuf[P_][:], func=AF.Copy,
                                     bias=0.25)
                        e.drain()
                        e.activation(out=gn2p[P_][:], in_=f25[P_][:], func=AF.Copy,
                                     bias=MAGIC)
                        e.drain()
                        e.activation(out=gn2p[P_][:], in_=gn2p[P_][:], func=AF.Copy,
                                     bias=-MAGIC)
                    b.stage("scalar", s_magic, reads=(f"f{P_}",),
                            writes=(f"n{P_}", f"f25_{P_}", "gn2x"))


                def g_mid(oc):
                    P_ = oc % 2
                    def s_rr(e, P_=P_):
                        e.tensor_tensor(out=nbuf[P_][:], in0=fbuf[P_][:], in1=nbuf[P_][:],
                                        op=ALU.subtract)
                        e.tensor_tensor(out=f25[P_][:], in0=f25[P_][:], in1=gn2p[P_][:],
                                        op=ALU.subtract)
                    b.stage("vector", s_rr,
                            reads=(f"f{P_}", f"n{P_}", f"f25_{P_}", "gn2x"),
                            writes=(f"n{P_}", f"f25_{P_}"))

                    def s_sincos(e, oc=oc, P_=P_):
                        e.activation(out=sin_bf[P_][:], in_=nbuf[P_][:], func=AF.Sin,
                                     scale=TWOPI)
                        e.activation(out=cos_bf[P_][:], in_=f25[P_][:], func=AF.Sin,
                                     scale=TWOPI)
                    b.stage("scalar", s_sincos, reads=(f"n{P_}", f"f25_{P_}"),
                            writes=(f"sin{P_}", f"cos{P_}"))


                def g_post(oc):
                    P_ = oc % 2
                    def s_aplanes(e, oc=oc, P_=P_):
                        for m in range(MM_):
                            e.tensor_tensor(out=are_bf[P_][:, m * T:(m + 1) * T],
                                            in0=cos_bf[P_][:, m * T:(m + 1) * T],
                                            in1=g_bf[P_][:], op=ALU.mult)
                            e.tensor_tensor(out=aim_bf[P_][:, m * T:(m + 1) * T],
                                            in0=sin_bf[P_][:, m * T:(m + 1) * T],
                                            in1=g_bf[P_][:], op=ALU.mult)
                    b.stage("vector", s_aplanes,
                            reads=(f"sin{P_}", f"cos{P_}", f"g{P_}"),
                            writes=(f"are{P_}", f"aim{P_}"))

                    def s_scan(e, oc=oc, P_=P_):
                        for m in range(MM_):
                            e.tensor_tensor_scan(
                                out=are_bf[P_][:, m * T:(m + 1) * T],
                                data0=are_bf[P_][:, m * T:(m + 1) * T],
                                data1=are_bf[P_][:, m * T:(m + 1) * T],
                                initial=col(ncar_re, oc * 4 + m),
                                op0=ALU.add, op1=ALU.bypass)
                            e.tensor_tensor_scan(
                                out=aim_bf[P_][:, m * T:(m + 1) * T],
                                data0=aim_bf[P_][:, m * T:(m + 1) * T],
                                data1=aim_bf[P_][:, m * T:(m + 1) * T],
                                initial=col(ncar_im, oc * 4 + m),
                                op0=ALU.add, op1=ALU.bypass)
                        e.drain()
                        e.tensor_copy(out=ncar_re[:, oc * 4:oc * 4 + 4],
                                      in_=are_bf[P_][:, T - 1::T])
                        e.tensor_copy(out=ncar_im[:, oc * 4:oc * 4 + 4],
                                      in_=aim_bf[P_][:, T - 1::T])
                    b.stage("vector", s_scan,
                            reads=(f"are{P_}", f"aim{P_}", "ncar"),
                            writes=(f"are{P_}", f"aim{P_}", "ncar", f"car{P_}"))

                    def s_sq(e, oc=oc, P_=P_):
                        e.activation(out=sin_bf[P_][:], in_=are_bf[P_][:], func=AF.Square)
                        e.activation(out=cos_bf[P_][:], in_=aim_bf[P_][:], func=AF.Square)
                    b.stage("scalar", s_sq, reads=(f"are{P_}", f"aim{P_}"),
                            writes=(f"sin{P_}", f"cos{P_}"))

                    def s_wpp(e, oc=oc, P_=P_):
                        yp = psY[:, oc * 256:oc * 256 + T]
                        for m in range(MM_):
                            e.matmul(yp, bd_sb[:, m * 128:(m + 1) * 128],
                                     are_bf[P_][:, m * T:(m + 1) * T],
                                     start=(m == 0), stop=False)
                        for m in range(MM_):
                            e.matmul(yp, bd_sb[:, (4 + m) * 128:(5 + m) * 128],
                                     aim_bf[P_][:, m * T:(m + 1) * T],
                                     start=False, stop=(m == 3))
                        qp = psQ[0:K, 0:T]
                        first = oc == 0
                        for m in range(MM_):
                            e.matmul(qp, gs_sb[:, oc * K:(oc + 1) * K],
                                     sin_bf[P_][:, m * T:(m + 1) * T],
                                     start=(first and m == 0), stop=False)
                        for m in range(MM_):
                            e.matmul(qp, gs_sb[:, oc * K:(oc + 1) * K],
                                     cos_bf[P_][:, m * T:(m + 1) * T],
                                     start=False, stop=(oc == 7 and m == 3))
                    b.stage("tensor", s_wpp,
                            reads=("bd", "gs", f"are{P_}", f"aim{P_}",
                                   f"sin{P_}", f"cos{P_}"),
                            writes=(f"psY{oc}", "psQ"))


                g_pre(0)
                g_mid(0)
                for oc in range(8):
                    if oc + 1 < 8:
                        g_pre(oc + 1)
                    g_post(oc)
                    if oc + 1 < 8:
                        g_mid(oc + 1)

                def s_sarg(e):
                    e.scalar_tensor_tensor(out=sarg_sb[:], in0=psQ[0:K, 0:T],
                                           scalar=1.0 / 256.0, in1=d2s_sb[:],
                                           op0=ALU.mult, op1=ALU.add)
                b.stage("vector", s_sarg, reads=("psQ", "dDx"), writes=("dDx",))

                def s_sln(e):
                    e.activation(out=sS_sb[:], in_=sarg_sb[:], func=AF.Ln)
                    e.drain()
                    e.activation(out=sS_sb[:], in_=sS_sb[:], func=AF.Exp, scale=-0.5)
                b.stage("scalar", s_sln, reads=("dDx",), writes=("dDx",))

                for oc in range(8):
                    P_ = oc % 2
                    psZn = "psZa" if P_ == 0 else "psZb"

                    def s_srep(e, oc=oc, P_=P_):
                        e.matmul(psRp[P_][:, 0:T],
                                 sel_sb[:, oc * 128:(oc + 1) * 128],
                                 sS_sb[:], start=True, stop=True)
                    b.stage("tensor", s_srep, reads=("sel", "dDx"), writes=(psZn,))

                    def s_uevc(e, oc=oc, P_=P_):
                        e.copy(out=ycp_bf[P_][:], in_=psY[:, oc * 256:oc * 256 + T])
                        e.drain()
                        e.copy(out=srep_bf[P_][:], in_=psRp[P_][:, 0:T])
                    b.stage("scalar", s_uevc,
                            reads=(f"psY{oc}", psZn),
                            writes=("ycp0", "srep0"))

                    def s_uev(e, oc=oc, P_=P_):
                        e.tensor_tensor(out=ycp_bf[P_][:], in0=ycp_bf[P_][:],
                                        in1=srep_bf[P_][:], op=ALU.mult)
                        e.drain()
                        e.tensor_tensor(out=gate_bf[oc][:], in0=ycp_bf[P_][:],
                                        in1=gate_bf[oc][:], op=ALU.mult)
                    b.stage("vector", s_uev,
                            reads=("ycp0", "srep0", "gate"),
                            writes=("ycp0", "gate", f"u{oc}"))

                for s in range(NSUB):
                    for nh in range(2):
                        def s_op(e, s=s, nh=nh):
                            pa = psY[:, (s * 2 + nh) * 512:(s * 2 + nh) * 512 + 512]
                            for fc in range(8):
                                e.matmul(pa, gate_bf[fc][:, s * 128:(s + 1) * 128],
                                         wout_bf[fc][:, nh * 512:(nh + 1) * 512],
                                         start=(fc == 0), stop=(fc == 7))
                        b.stage("tensor", s_op, reads=tuple(f"u{i}" for i in range(8)) + ("wout",), writes=(f"psY{(s * 2 + nh) * 2}", f"psY{(s * 2 + nh) * 2 + 1}"))

                def s_res1_0(e, tpar=tpar):
                    e.tensor_tensor(
                        out=xo_sb[0][:, 0 * 512:(0 + 1) * 512],
                        in0=x_sb[tpar][0][:, 0 * 512:(0 + 1) * 512],
                        in1=psY[:, 0 * 512:0 * 512 + 512],
                        op=ALU.add)
                b.stage("vector", s_res1_0,
                        reads=(f"x{tpar}0", "psY0", "psY1"),
                        writes=("xo0",))

                def s_res1_1(e, tpar=tpar):
                    e.tensor_tensor(
                        out=xo_sb[0][:, 1 * 512:(1 + 1) * 512],
                        in0=x_sb[tpar][0][:, 1 * 512:(1 + 1) * 512],
                        in1=psY[:, 1 * 512:1 * 512 + 512],
                        op=ALU.add)
                b.stage("vector", s_res1_1,
                        reads=(f"x{tpar}0", "psY2", "psY3"),
                        writes=("xo0",))

                def s_res1_2(e, tpar=tpar):
                    e.tensor_tensor(
                        out=xo_sb[1][:, 0 * 512:(0 + 1) * 512],
                        in0=x_sb[tpar][1][:, 0 * 512:(0 + 1) * 512],
                        in1=psY[:, 2 * 512:2 * 512 + 512],
                        op=ALU.add)
                b.stage("vector", s_res1_2,
                        reads=(f"x{tpar}1", "psY4", "psY5"),
                        writes=("xo1",))

                def s_res1_3(e, tpar=tpar):
                    e.tensor_tensor(
                        out=xo_sb[1][:, 1 * 512:(1 + 1) * 512],
                        in0=x_sb[tpar][1][:, 1 * 512:(1 + 1) * 512],
                        in1=psY[:, 3 * 512:3 * 512 + 512],
                        op=ALU.add)
                b.stage("vector", s_res1_3,
                        reads=(f"x{tpar}1", "psY6", "psY7"),
                        writes=("xo1",))

                def s_stx(e, sem, r0=r0):
                    for s in range(NSUB):
                        e.dma_start(out=xmid.ap()[r0 + s * 128:r0 + (s + 1) * 128, :],
                                    in_=xo_sb[s][:]).then_inc(sem, 16)
                b.dma(s_stx, NSUB, reads=("xo0", "xo1"), writes=("xmid",))


            emit_headA(0)
            emit_headB(0)
            for ti in range(NT):
                if ti + 1 < NT:
                    emit_headA(ti + 1)
                emit_tail(ti)
                if ti + 1 < NT:
                    emit_headB(ti + 1)

        with contextlib.ExitStack() as SB:
            w1q = [sb(f"w1q{i}", (128, FF), BF16, SB) for i in range(8)]
            w2q = [sb(f"w2q{i}", (128, D), BF16, SB) for i in range(32)]
            wstage = sb("wstage", (128, 512), F32, SB)
            qc = sb("qc", (128, 10), F32, SB)
            h_act = sb("h_act", (128, FF), BF16, SB)
            h2_sb = sb("h2_sb", (128, D), F32, SB)
            q1_bf = sb("q1_bf", (128, D), BF16, SB)
            q1f = [sb(f"q1f{i}", (128, 128), BF16, SB) for i in range(8)]
            q2f = [sb(f"q2f{i}", (128, 128), BF16, SB) for i in range(16)]
            c2p = [sb(f"c2_{i}", (128, 20), F32, SB) for i in range(4)]
            iacol1 = [sb(f"iacol1_{i}", (128, 1), BF16, SB) for i in range(4)]
            iacol2 = [sb(f"iacol2_{i}", (128, 1), BF16, SB) for i in range(4)]
            iarp = [sb(f"iar_{i}", (1, 128), BF16, SB) for i in range(4)]
            bnst = sb("bnst", (128, 48), F32, SB)
            bnst2 = sb("bnst2", (128, 48), F32, SB)
            amx = sb("amx", (128, 8), F32, SB)
            al1 = [sb(f"al1_{i}", (128, 1), F32, SB) for i in range(8)]
            al2 = [sb(f"al2_{i}", (128, 1), F32, SB) for i in range(8)]
            psT2a = SB.enter_context(nc.psum_tensor("psT2a", [128, 1024], BF16))
            psT2b = SB.enter_context(nc.psum_tensor("psT2b", [128, 1024], BF16))
            psT2 = [psT2a, psT2b]

            def quant_weights(tensors, dram_name, nch, width, scale_slot, do_gb):
                pfx = dram_name
                for fc in range(nch):
                    b.dma(lambda e, sem, fc=fc:
                          e.dma_start(out=tensors[fc][:],
                                      in_=ap[dram_name][fc * 128:(fc + 1) * 128, :])
                          .then_inc(sem, 16), 1, eng="gpsimd",
                          writes=(f"{pfx}_{fc}",))

                ngrp = (nch + 7) // 8
                for gqi in range(ngrp):
                    def s_m1(e, gqi=gqi):
                        for j in range(8):
                            fc = gqi * 8 + j
                            if fc < nch:
                                e.tensor_reduce(out=col(qc, j),
                                                in_=tensors[fc][:], axis=AX,
                                                op=ALU.add,
                                                apply_absolute_value=True)
                    b.stage("vector", s_m1,
                            reads=tuple(f"{pfx}_{gqi * 8 + j}" for j in range(8)
                                        if gqi * 8 + j < nch),
                            writes=("qc",))

                    def s_m2(e, gqi=gqi):
                        e.matmul(psQ[0:1, gqi * 8:gqi * 8 + 8], ones_c[:],
                                 qc[:, 0:8], start=True, stop=True)
                    b.stage("tensor", s_m2, reads=("qc", "ones"), writes=("psQ",))

                def s_m3(e):
                    e.reduce_sum(out=srow[0:1, 0:1], in_=psQ[0:1, 0:nch], axis=AX)
                    e.drain()
                    e.tensor_scalar(out=srow[0:1, 1:2], in0=srow[0:1, 0:1],
                                    scalar1=1.0 / (nch * 128 * width),
                                    scalar2=1e-5, op0=ALU.mult, op1=ALU.max)
                    e.drain()
                    e.reciprocal(out=srow[0:1, 2:3], in_=srow[0:1, 1:2])
                b.stage("vector", s_m3, reads=("psQ",), writes=("srow",))

                def s_bc(e):
                    e.matmul(psQ[:, 8:10], ones_r[:], srow[0:1, 1:3],
                             start=True, stop=True)
                b.stage("tensor", s_bc, reads=("srow", "ones"), writes=("psQ",))

                def s_bc2(e):
                    e.tensor_copy(out=smallc[:, scale_slot:scale_slot + 2],
                                  in_=psQ[:, 8:10])
                b.stage("vector", s_bc2, reads=("psQ",), writes=("smallc",))

                nseg = (width + 511) // 512
                segw = width // nseg
                for fc in range(nch):
                    for sg in range(nseg):
                        def s_q(e, fc=fc, sg=sg):
                            w0 = sg * segw
                            ws = wstage[:, 0:segw]
                            e.tensor_copy(out=ws, in_=tensors[fc][:, w0:w0 + segw])
                            e.drain()
                            e.tensor_scalar(out=ws, in0=ws,
                                            scalar1=col(smallc, scale_slot + 1),
                                            scalar2=MAGIC, op0=ALU.mult, op1=ALU.add)
                            e.drain()
                            e.tensor_scalar(out=ws, in0=ws, scalar1=MAGIC,
                                            scalar2=None, op0=ALU.subtract)
                            e.drain()
                            e.tensor_scalar(out=ws, in0=ws, scalar1=1.0,
                                            scalar2=-1.0, op0=ALU.min, op1=ALU.max)
                            e.drain()
                            if do_gb:
                                e.tensor_tensor(out=ws, in0=ws,
                                                in1=gbB[:, w0:w0 + segw] if segw < D else gbB[:],
                                                op=ALU.mult)
                                e.drain()
                            e.tensor_copy(out=tensors[fc][:, w0:w0 + segw], in_=ws)
                        b.stage("vector", s_q,
                                reads=(f"{pfx}_{fc}", "smallc", "gbB"),
                                writes=(f"{pfx}_{fc}", "wstage"))

            b.barrier()
            quant_weights(w1q, "w1T", 8, FF, 0, False)
            quant_weights(w2q, "w2T", 32, D, 2, True)

            I32 = mybir.dt.int32

            def qrsqrt(e, c2, dst, src):
                d_i = col(c2, dst).bitcast(I32)
                e.tensor_scalar(out=d_i, in0=col(c2, src).bitcast(I32),
                                scalar1=1, scalar2=None,
                                op0=ALU.logical_shift_right)
                e.tensor_scalar(out=col(c2, 1), in0=col(c2, src),
                                scalar1=-0.5, scalar2=None, op0=ALU.mult)
                e.drain()
                e.tensor_scalar(out=d_i, in0=d_i, scalar1=-1,
                                scalar2=0x5F3759DF, op0=ALU.mult, op1=ALU.add)
                e.drain()
                for _ in range(2):
                    e.tensor_tensor(out=col(c2, 0), in0=col(c2, dst),
                                    in1=col(c2, dst), op=ALU.mult)
                    e.drain()
                    e.tensor_scalar(out=col(c2, 0), in0=col(c2, 0),
                                    scalar1=col(c2, 1), scalar2=1.5,
                                    op0=ALU.mult, op1=ALU.add)
                    e.drain()
                    e.tensor_tensor(out=col(c2, dst), in0=col(c2, dst),
                                    in1=col(c2, 0), op=ALU.mult)
                    e.drain()

            def p2_front(u):
                ti, s = divmod(u, NSUB)
                r0 = ti * T
                tpar = ti % 2
                up = u % 4
                u8 = u % 8
                c2 = c2p[up]
                cn = f"c2_{up}"
                ia_row = iarp[up]
                ian = f"ia{up}"

                def s_ldm(e, sem, r0=r0, s=s, tpar=tpar):
                    e.dma_start(out=x_sb[tpar][s][:],
                                in_=xmid.ap()[r0 + s * 128:r0 + (s + 1) * 128, :]) \
                        .then_inc(sem, 16)
                b.dma(s_ldm, 1, reads=("xmid",), writes=(f"x{tpar}{s}",))

                def s_ln2(e, s=s, tpar=tpar, c2=c2):
                    e.bn_stats(out=bnst[:, 0:6], in_=x_sb[tpar][s][:, 0:512])
                    e.bn_stats(out=bnst[:, 6:12], in_=x_sb[tpar][s][:, 512:1024])
                    e.drain()
                    e.bn_aggr(out=c2[:, 2:4], in_=bnst[:, 0:12])
                    e.drain()
                    e.tensor_scalar(out=col(c2, 4), in0=col(c2, 3),
                                    scalar1=EPS, scalar2=None, op0=ALU.add)
                    e.drain()
                    qrsqrt(e, c2, 5, 4)
                b.stage("vector", s_ln2, reads=(f"x{tpar}{s}",),
                        writes=(cn, "bnst"))

                def s_h2(e, s=s, tpar=tpar, c2=c2):
                    e.tensor_scalar(out=h2_sb[:], in0=x_sb[tpar][s][:],
                                    scalar1=col(c2, 2), scalar2=col(c2, 5),
                                    op0=ALU.subtract, op1=ALU.mult)
                    e.drain()
                    e.tensor_tensor(out=h2_sb[:], in0=h2_sb[:], in1=cb1B[:],
                                    op=ALU.mult)
                    e.drain()
                    e.tensor_tensor(out=h2_sb[:], in0=h2_sb[:], in1=sbB[:],
                                    op=ALU.add)
                    e.drain()
                    e.bn_stats(out=bnst[:, 0:6], in_=h2_sb[:, 0:512])
                    e.bn_stats(out=bnst[:, 6:12], in_=h2_sb[:, 512:1024])
                    e.reduce_max(out=col(c2, 7), in_=h2_sb[:], axis=AX,
                                 apply_absolute_value=True)
                    e.drain()
                    e.bn_aggr(out=c2[:, 17:19], in_=bnst[:, 0:12])
                    e.drain()
                    e.tensor_tensor(out=col(c2, 6), in0=col(c2, 17),
                                    in1=col(c2, 17), op=ALU.mult)
                    e.drain()
                    e.tensor_tensor(out=col(c2, 6), in0=col(c2, 6),
                                    in1=col(c2, 18), op=ALU.add)
                    e.drain()
                    e.tensor_scalar(out=col(c2, 6), in0=col(c2, 6),
                                    scalar1=EPS, scalar2=None, op0=ALU.add)
                    e.drain()
                    qrsqrt(e, c2, 8, 6)
                b.stage("vector", s_h2,
                        reads=(f"x{tpar}{s}", cn, "cb1B", "sbB"),
                        writes=("h2", "bnst", cn))

                def s_cs1(e, c2=c2):
                    e.tensor_tensor(out=col(c2, 9), in0=col(c2, 7),
                                    in1=col(c2, 8), op=ALU.mult)
                    e.drain()
                    e.tensor_scalar(out=col(c2, 9), in0=col(c2, 9),
                                    scalar1=1e-5, scalar2=None, op0=ALU.max)
                    e.drain()
                    e.reciprocal(out=col(c2, 10), in_=col(c2, 9))
                    e.drain()
                    e.scalar_tensor_tensor(out=col(c2, 11), in0=col(c2, 8),
                                           scalar=127.0, in1=col(c2, 10),
                                           op0=ALU.mult, op1=ALU.mult)
                    e.drain()
                    e.reciprocal(out=col(c2, 12), in_=col(c2, 11))
                    e.drain()
                    e.reciprocal(out=col(c2, 16), in_=col(c2, 8))
                    e.drain()
                    e.scalar_tensor_tensor(out=al1[u8][:, 0:1], in0=col(c2, 12),
                                           scalar=col(smallc, 0),
                                           in1=col(c2, 8),
                                           op0=ALU.mult, op1=ALU.mult)
                    e.scalar_tensor_tensor(out=col(c2, 14), in0=col(c2, 11),
                                           scalar=col(smallc, 1),
                                           in1=col(c2, 16),
                                           op0=ALU.mult, op1=ALU.mult)
                    e.drain()
                    e.tensor_copy(out=iacol1[up][:], in_=col(c2, 14))
                b.stage("vector", s_cs1, reads=(cn, "smallc"),
                        writes=(cn, f"al1_{u8}", f"iac1{up}"))

                def s_q1(e, c2=c2, up=up):
                    e.tensor_scalar(out=h2_sb[:], in0=h2_sb[:],
                                    scalar1=col(c2, 11), scalar2=MAGIC,
                                    op0=ALU.mult, op1=ALU.add)
                    e.drain()
                    e.tensor_scalar(out=q1_bf[:], in0=h2_sb[:],
                                    scalar1=MAGIC, scalar2=None,
                                    op0=ALU.subtract)
                b.stage("vector", s_q1, reads=("h2", cn),
                        writes=("q1", "h2"))

                def s_iat(e, up=up):
                    e.transpose(psT[0:1, 0:128], iacol1[up][:], ident[:])
                b.stage("tensor", s_iat, reads=(f"iac1{up}", "ident"), writes=("psT",))

                def s_iaev(e, ia_row=ia_row):
                    e.activation(out=ia_row[0:1, :], in_=psT[0:1, 0:128],
                                 func=AF.Copy)
                b.stage("scalar", s_iaev, reads=("psT",), writes=(ian,))

                def s_tq1(e):
                    for fc in range(8):
                        e.transpose(psT[:, fc * 128:(fc + 1) * 128],
                                    q1_bf[:, fc * 128:(fc + 1) * 128], ident[:])
                b.stage("tensor", s_tq1, reads=("q1", "ident"), writes=("psT",))

                def s_tq1e(e):
                    for fc in range(8):
                        e.activation(out=q1f[fc][:],
                                     in_=psT[:, fc * 128:(fc + 1) * 128],
                                     func=AF.Copy)
                b.stage("scalar", s_tq1e, reads=("psT",), writes=("q1f",))

            def p2_mid(u):
                up = u % 4
                u8 = u % 8
                c2 = c2p[up]
                cn = f"c2_{up}"
                ia_row = iarp[up]
                ian = f"ia{up}"
                for nn in range(8):
                    def s_w1mm(e, nn=nn, ia_row=ia_row):
                        ph = psY[:, (nn % 2) * 512:(nn % 2) * 512 + 512]
                        for fc in range(8):
                            e.matmul(ph, q1f[fc][:],
                                     w1q[fc][:, nn * 512:(nn + 1) * 512],
                                     start=(fc == 0), stop=False)
                        e.matmul(ph, ia_row[0:1, :],
                                 b1_r[0:1, nn * 512:(nn + 1) * 512],
                                 start=False, stop=True)
                    b.stage("tensor", s_w1mm,
                            reads=("q1f", ian, "b1r") + tuple(f"w1T_{i}" for i in range(8)),
                            writes=(("psY0", "psY1") if nn % 2 == 0 else ("psY2", "psY3")))

                    def s_gelu(e, nn=nn, u8=u8):
                        e.activation(out=h_act[:, nn * 512:(nn + 1) * 512],
                                     in_=psY[:, (nn % 2) * 512:(nn % 2) * 512 + 512],
                                     func=AF.Gelu, scale=al1[u8][:, 0:1])
                    b.stage("scalar", s_gelu,
                            reads=(("psY0", "psY1") if nn % 2 == 0 else ("psY2", "psY3")) + (f"al1_{u8}",),
                            writes=(f"ha{nn}",))

            def p2_back(u):
                ti, s = divmod(u, NSUB)
                r0 = ti * T
                tpar = ti % 2
                up = u % 4
                u8 = u % 8
                c2 = c2p[up]
                cn = f"c2_{up}"
                ia_row = iarp[up]
                ian = f"ia{up}"

                for qq in range(8):
                    def s_st2q(e, qq=qq):
                        e.bn_stats(out=bnst2[:, qq * 6:(qq + 1) * 6],
                                   in_=h_act[:, qq * 512:(qq + 1) * 512])
                        e.reduce_max(out=amx[:, qq:qq + 1],
                                     in_=h_act[:, qq * 512:(qq + 1) * 512],
                                     axis=AX, apply_absolute_value=True)
                    b.stage("vector", s_st2q, reads=(f"ha{qq}",),
                            writes=("bnst2", "amx"))

                def s_st2(e, c2=c2):
                    e.reduce_max(out=col(c2, 7), in_=amx[:, 0:8], axis=AX)
                    e.bn_aggr(out=c2[:, 17:19], in_=bnst2[:, 0:48])
                    e.drain()
                    e.tensor_tensor(out=col(c2, 6), in0=col(c2, 17),
                                    in1=col(c2, 17), op=ALU.mult)
                    e.drain()
                    e.tensor_tensor(out=col(c2, 6), in0=col(c2, 6),
                                    in1=col(c2, 18), op=ALU.add)
                    e.drain()
                    e.tensor_scalar(out=col(c2, 6), in0=col(c2, 6),
                                    scalar1=EPS, scalar2=None, op0=ALU.add)
                    e.drain()
                    qrsqrt(e, c2, 8, 6)
                b.stage("vector", s_st2, reads=("bnst2", "amx"), writes=(cn,))

                def s_cs2(e, c2=c2):
                    e.tensor_tensor(out=col(c2, 9), in0=col(c2, 7),
                                    in1=col(c2, 8), op=ALU.mult)
                    e.drain()
                    e.tensor_scalar(out=col(c2, 9), in0=col(c2, 9),
                                    scalar1=1e-5, scalar2=None, op0=ALU.max)
                    e.drain()
                    e.reciprocal(out=col(c2, 10), in_=col(c2, 9))
                    e.drain()
                    e.scalar_tensor_tensor(out=col(c2, 11), in0=col(c2, 8),
                                           scalar=127.0, in1=col(c2, 10),
                                           op0=ALU.mult, op1=ALU.mult)
                    e.drain()
                    e.reciprocal(out=col(c2, 12), in_=col(c2, 11))
                    e.drain()
                    e.reciprocal(out=col(c2, 16), in_=col(c2, 8))
                    e.drain()
                    e.scalar_tensor_tensor(out=al2[u8][:, 0:1], in0=col(c2, 12),
                                           scalar=col(smallc, 2),
                                           in1=col(c2, 8),
                                           op0=ALU.mult, op1=ALU.mult)
                    e.scalar_tensor_tensor(out=col(c2, 14), in0=col(c2, 11),
                                           scalar=col(smallc, 3),
                                           in1=col(c2, 16),
                                           op0=ALU.mult, op1=ALU.mult)
                    e.drain()
                    e.tensor_copy(out=iacol2[up][:], in_=col(c2, 14))
                b.stage("vector", s_cs2, reads=(cn, "smallc"),
                        writes=(cn, f"al2_{u8}", f"iac2{up}"))

                for hh in range(8):
                    def s_q2h(e, c2=c2, hh=hh):
                        hs = hh * 512
                        e.tensor_copy(out=wstage[:], in_=h_act[:, hs:hs + 512])
                        e.drain()
                        e.tensor_scalar(out=wstage[:], in0=wstage[:],
                                        scalar1=col(c2, 11), scalar2=MAGIC,
                                        op0=ALU.mult, op1=ALU.add)
                        e.drain()
                        e.tensor_scalar(out=h_act[:, hs:hs + 512], in0=wstage[:],
                                        scalar1=MAGIC, scalar2=None,
                                        op0=ALU.subtract)
                    b.stage("vector", s_q2h, reads=(f"ha{hh}", cn),
                            writes=(f"ha{hh}", "wstage"))

                def s_ia2t(e, up=up):
                    e.transpose(psT2a[0:1, 0:128], iacol2[up][:], ident[:])
                b.stage("tensor", s_ia2t, reads=(f"iac2{up}", "ident"),
                        writes=("psT2a",))

                def s_ia2e(e, ia_row=ia_row):
                    e.activation(out=ia_row[0:1, :], in_=psT2a[0:1, 0:128],
                                 func=AF.Copy)
                b.stage("scalar", s_ia2e, reads=("psT2a",), writes=(ian,))

                for half in range(2):
                    for rr in range(2):
                        gb = half * 2 + rr
                        pst = psT2[gb % 2]
                        pstn = f"psT2{'ab'[gb % 2]}"

                        def s_tq2(e, half=half, rr=rr, pst=pst):
                            for j in range(8):
                                fc = half * 16 + rr * 8 + j
                                e.transpose(pst[:, j * 128:(j + 1) * 128],
                                            h_act[:, fc * 128:(fc + 1) * 128],
                                            ident[:])
                        b.stage("tensor", s_tq2,
                                reads=(f"ha{half * 4 + rr * 2}",
                                       f"ha{half * 4 + rr * 2 + 1}", "ident"),
                                writes=(pstn,))

                        def s_tq2e(e, half=half, rr=rr, pst=pst):
                            for j in range(8):
                                lc = rr * 8 + j
                                e.activation(out=q2f[lc][:],
                                             in_=pst[:, j * 128:(j + 1) * 128],
                                             func=AF.Copy)
                        b.stage("scalar", s_tq2e, reads=(pstn,), writes=("q2f",))

                    def s_w2mm(e, half=half, ia_row=ia_row):
                        for nn in range(2):
                            ph = psY[:, (2 + nn) * 512:(2 + nn) * 512 + 512]
                            for lc in range(16):
                                fc = half * 16 + lc
                                e.matmul(ph, q2f[lc][:],
                                         w2q[fc][:, nn * 512:(nn + 1) * 512],
                                         start=(fc == 0), stop=False)
                            if half == 1:
                                e.matmul(ph, ia_row[0:1, :],
                                         b2gb_r[0:1, nn * 512:(nn + 1) * 512],
                                         start=False, stop=True)
                    b.stage("tensor", s_w2mm,
                            reads=("q2f", ian, "b2gbr") +
                            tuple(f"w2T_{i}" for i in range(32)),
                            writes=("psY4", "psY5", "psY6", "psY7"))

                def s_res2(e, s=s, tpar=tpar, u8=u8):
                    for nn in range(2):
                        e.scalar_tensor_tensor(
                            out=xo_sb[s][:, nn * 512:(nn + 1) * 512],
                            in0=psY[:, (2 + nn) * 512:(2 + nn) * 512 + 512],
                            scalar=al2[u8][:, 0:1],
                            in1=x_sb[tpar][s][:, nn * 512:(nn + 1) * 512],
                            op0=ALU.mult, op1=ALU.add)
                b.stage("vector", s_res2,
                        reads=(f"x{tpar}{s}", f"al2_{u8}",
                               "psY4", "psY5", "psY6", "psY7"),
                        writes=(f"xo{s}",))

                def s_sto(e, sem, r0=r0, s=s):
                    e.dma_start(out=out_d.ap()[r0 + s * 128:r0 + (s + 1) * 128, :],
                                in_=xo_sb[s][:]).then_inc(sem, 16)
                b.dma(s_sto, 1, reads=(f"xo{s}",), writes=("outd",))

            NU = NT * NSUB
            p2_front(0)
            p2_mid(0)
            for u in range(NU):
                if u + 1 < NU:
                    p2_front(u + 1)
                p2_back(u)
                if u + 1 < NU:
                    p2_mid(u + 1)

        b.finish()

        with nc.Block() as block:
            b.emit(block)

    return nc


def _host_consts(L):
    ident = np.eye(128, dtype=np.float32)
    import ml_dtypes
    ident = ident.astype(ml_dtypes.bfloat16)
    ones_row = np.ones((1, 128), dtype=np.float32)
    sel = np.zeros((K, 8 * 128), dtype=np.float32)
    for oc in range(8):
        for p in range(128):
            sel[4 * oc + p // 32, oc * 128 + p] = 1.0
    gs = np.zeros((128, 8 * K), dtype=np.float32)
    for oc in range(8):
        for kl in range(4):
            for h in range(32):
                gs[kl * 32 + h, oc * K + 4 * oc + kl] = 1.0
    gs = gs.astype(ml_dtypes.bfloat16)
    iota = np.arange(L, dtype=np.float32)
    iotaD = np.zeros((K, L), dtype=np.float32)
    iotaD[0:28, :] = (L - 1) - iota
    iotaD[28:32, :] = iota
    return {"ident": ident, "ident_f": np.eye(128, dtype=np.float32),
            "ones_row": ones_row,
            "ones_col": np.ones((128, 1), dtype=np.float32),
            "Sel": sel, "GS": gs, "iotaD": iotaD}


def make_inputs(xrow, crow, in_proj_w, conv_w, conv_b, theta, decay, anchor,
                score_scale, deriv_logits, norm_scale, W_re, W_im, out_proj_w,
                w1, b1, w2, b2, wmod, L):
    th = np.asarray(theta).reshape(K, H, MM_)
    th_all = np.zeros((128, K), dtype=np.float32)
    for oc in range(8):
        for p in range(128):
            kl, h = p // 32, p % 32
            for m in range(MM_):
                th_all[p, oc * 4 + m] = th[4 * oc + kl, h, m]
    cw = np.asarray(conv_w).reshape(4, TOT)
    wconv = np.zeros((128, 4 * NC17), dtype=np.float32)
    cbv = np.zeros((128, NC17), dtype=np.float32)
    for oc in range(NC17):
        w = CHUNK[oc]
        for d in range(4):
            wconv[0:w, oc * 4 + d] = cw[d, oc * 128:oc * 128 + w]
        cbv[0:w, oc] = np.asarray(conv_b)[oc * 128:oc * 128 + w]
    nsBD = np.zeros((128, 8), dtype=np.float32)
    ns = np.asarray(norm_scale)
    for j in range(4):
        for h in range(32):
            p = 32 * j + h
            for m in range(MM_):
                nsBD[p, m] = ns[h * MM_ + m]
                nsBD[p, 4 + m] = ns[128 + h * MM_ + m]
    BDraw = np.zeros((128, 8 * 128), dtype=np.float32)
    Wre = np.asarray(W_re)
    Wim = np.asarray(W_im)
    for m in range(MM_):
        for j in range(4):
            for h in range(32):
                BDraw[32 * j + h, m * 128 + 32 * j:m * 128 + 32 * j + 32] = \
                    Wre[h * MM_ + m, :]
                BDraw[32 * j + h, (4 + m) * 128 + 32 * j:(4 + m) * 128 + 32 * j + 32] = \
                    Wim[h * MM_ + m, :]
    slopes = np.concatenate([np.asarray(decay).reshape(-1),
                             np.asarray(anchor).reshape(-1)]).reshape(K, 1)
    d = {
        "x": np.ascontiguousarray(xrow, dtype=np.float32),
        "in_proj_w": np.ascontiguousarray(in_proj_w, dtype=np.float32),
        "out_proj_w": np.ascontiguousarray(out_proj_w, dtype=np.float32),
        "w1T": np.ascontiguousarray(np.asarray(w1).T, dtype=np.float32),
        "w2T": np.ascontiguousarray(np.asarray(w2).T, dtype=np.float32),
        "b1": np.ascontiguousarray(np.asarray(b1).reshape(1, FF), dtype=np.float32),
        "b2": np.ascontiguousarray(np.asarray(b2).reshape(1, D), dtype=np.float32),
        "c_cols": np.ascontiguousarray(np.asarray(crow).reshape(8, 128).T,
                                       dtype=np.float32),
        "wmod": np.ascontiguousarray(wmod, dtype=np.float32),
        "wconv_cols": wconv,
        "convb_cols": cbv,
        "th_all": th_all,
        "ss_col": np.ascontiguousarray(np.asarray(score_scale).reshape(K, 1),
                                       dtype=np.float32),
        "slopes_col": np.ascontiguousarray(slopes, dtype=np.float32),
        "dl_row": np.ascontiguousarray(np.asarray(deriv_logits).reshape(1, 3),
                                       dtype=np.float32),
        "nsBD": nsBD,
        "BDraw": BDraw,
    }
    d.update(_host_consts(L))
    return d


_NC_CACHE = {}


def kernel(L0_attn_x, L0_attn_in_proj_w, L0_attn_conv_w, L0_attn_conv_b,
           L0_attn_theta, L0_attn_decay_slopes, L0_attn_anchor_slopes,
           L0_attn_score_scale, L0_attn_deriv_logits, L0_attn_norm_scale,
           L0_attn_W_re, L0_attn_W_im, L0_attn_out_proj_w,
           L0_ffn_w1, L0_ffn_b1, L0_ffn_w2, L0_ffn_b2, L0_c, L0_wmod):
    L = 4096
    if L not in _NC_CACHE:
        _NC_CACHE[L] = build_nc(L=L, T=256)
    nc = _NC_CACHE[L]
    x = np.ascontiguousarray(np.asarray(L0_attn_x), dtype=np.float32)
    in_maps = []
    for bi in range(8):
        in_maps.append(make_inputs(
            x[bi], np.asarray(L0_c)[bi], L0_attn_in_proj_w, L0_attn_conv_w,
            L0_attn_conv_b, L0_attn_theta, L0_attn_decay_slopes,
            L0_attn_anchor_slopes, L0_attn_score_scale, L0_attn_deriv_logits,
            L0_attn_norm_scale, L0_attn_W_re, L0_attn_W_im, L0_attn_out_proj_w,
            L0_ffn_w1, L0_ffn_b1, L0_ffn_w2, L0_ffn_b2, L0_wmod, L))
    res = run_bass_kernel_spmd(nc, in_maps, core_ids=list(range(8)))
    out = np.stack([res.results[bi]["out"] for bi in range(8)], axis=0)
    return out.astype(np.float32)
